# version 02
# baseline (speedup 1.0000x reference)
import numpy as np
import ml_dtypes
from contextlib import ExitStack

import concourse.bass as bass
import concourse.mybir as mybir
from concourse.bass_utils import run_bass_kernel_spmd

F32 = mybir.dt.float32
BF16 = mybir.dt.bfloat16
AF = mybir.ActivationFunctionType
ALU = mybir.AluOpType
AX = mybir.AxisListType
BF = ml_dtypes.bfloat16

S = 2048
D = 1024
H = 16
HD = 64
FFD = 4096
NE = 8
NCH = 8
NTC = 4
MW = 2560
EPS = 1e-5

ENGS = ("tensor", "vector", "scalar", "gpsimd", "sync")


class Sem:
    def __init__(self, h, step=1):
        self.h = h
        self.step = step
        self.n = 0


class Rec:
    def __init__(self):
        self.q = {e: [] for e in ENGS}
        self.waited = {e: {} for e in ENGS}

    def op(self, eng, fn, inc=None):
        if inc is not None:
            inc.n += inc.step
            h, stp = inc.h, inc.step
            self.q[eng].append(lambda e, fn=fn, h=h, stp=stp: fn(e).then_inc(h, stp))
            return inc.n
        self.q[eng].append(lambda e, fn=fn: fn(e))
        return None

    def wait(self, eng, sem, val=None):
        v = sem.n if val is None else val
        if v <= 0:
            return
        if self.waited[eng].get(id(sem), -1) >= v:
            return
        self.waited[eng][id(sem)] = v
        self.q[eng].append(lambda e, h=sem.h, v=v: e.wait_ge(h, v))

    def replay(self, block):
        rq = self.q

        @block.tensor
        def _(e):
            for f in rq["tensor"]:
                f(e)

        @block.vector
        def _(e):
            for f in rq["vector"]:
                f(e)

        @block.scalar
        def _(e):
            for f in rq["scalar"]:
                f(e)

        @block.gpsimd
        def _(e):
            for f in rq["gpsimd"]:
                f(e)

        @block.sync
        def _(e):
            for f in rq["sync"]:
                f(e)


def _host_constants():
    c = {}
    pos = np.arange(S - 1, -1, -1, dtype=np.float64)
    inv_freq = 1.0 / 10000.0 ** (np.arange(0, D, 2, dtype=np.float64) / D)
    sinus = pos[:, None] * inv_freq[None, :]
    pe = np.concatenate([np.sin(sinus), np.cos(sinus)], axis=-1)
    c["c_pet"] = np.ascontiguousarray(pe.T).astype(np.float32)
    ang = np.arange(S, dtype=np.float64)[:, None] * (
        1.0 / 10000.0 ** (np.arange(0, HD, 2, dtype=np.float64) / HD))[None, :]
    cos64 = np.repeat(np.cos(ang), 2, axis=1).T
    sin64 = np.repeat(np.sin(ang), 2, axis=1).T
    sgn = np.where(np.arange(HD) % 2 == 0, -1.0, 1.0)[:, None]
    sinm64 = sin64 * sgn
    c["c_cos"] = np.vstack([cos64, cos64]).astype(BF)
    c["c_sinm"] = np.vstack([sinm64, sinm64]).astype(BF)
    c["c_id"] = np.eye(128, dtype=np.float32)
    pswap = np.zeros((128, 128), dtype=np.float32)
    for i in range(0, 128, 2):
        pswap[i + 1, i] = 1.0
        pswap[i, i + 1] = 1.0
    c["c_pswap"] = pswap.astype(BF)
    eye8 = np.zeros((NE, NE * 128), dtype=np.float32)
    for e in range(NE):
        eye8[e, e * 128:(e + 1) * 128] = 1.0
    c["c_eye8"] = eye8.astype(BF)
    return c


def _col8(v):
    return np.ascontiguousarray(v.reshape(8, 128).T)


def _prep_shared(w):
    m = {}
    m.update(_host_constants())
    m["WqT"] = np.ascontiguousarray(np.asarray(w["L0_attn_Wq"]).T)
    m["WkT"] = np.ascontiguousarray(np.asarray(w["L0_attn_Wk"]).T)
    m["WvT"] = np.ascontiguousarray(np.asarray(w["L0_attn_Wv"]).T)
    m["WoaT"] = np.ascontiguousarray(np.asarray(w["L0_attn_Wo"]).T)
    m["boc"] = _col8(np.asarray(w["L0_attn_bo"]))
    for k in ("Wqkv", "Wo", "Wr", "Wff1", "Wff2", "Wg", "We1", "We2", "Wout"):
        m[k] = np.ascontiguousarray(np.asarray(w["L0_ffn_" + k]))
    m["uc"] = _col8(np.asarray(w["L0_ffn_u_bias"]).reshape(-1))
    m["vc"] = _col8(np.asarray(w["L0_ffn_v_bias"]).reshape(-1))
    for i in (1, 2, 3):
        m[f"ln{i}g"] = _col8(np.asarray(w[f"L0_ffn_ln{i}_g"]))
        m[f"ln{i}b"] = _col8(np.asarray(w[f"L0_ffn_ln{i}_b"]))
    m["bff1c"] = np.ascontiguousarray(np.asarray(w["L0_ffn_bff1"]).reshape(32, 128).T)
    m["bff2c"] = _col8(np.asarray(w["L0_ffn_bff2"]))
    be1 = np.asarray(w["L0_ffn_be1"])
    m["be1c"] = np.ascontiguousarray(
        be1.reshape(NE, 32, 128).transpose(2, 0, 1).reshape(128, NE * 32))
    be2 = np.asarray(w["L0_ffn_be2"])
    m["be2c"] = np.ascontiguousarray(
        be2.reshape(NE, 8, 128).transpose(2, 0, 1).reshape(128, NE * 8))
    m["be2r"] = np.ascontiguousarray(be2)
    return m


def _build(taps=(), stages=3):
    taps = set(taps)
    nc = bass.Bass("TRN2", target_bir_lowering=False)

    def dram_in(name, shape, dtype=F32):
        return nc.dram_tensor(name, list(shape), dtype, kind="ExternalInput").ap()

    def dram_int(name, shape, dtype=F32):
        kind = "ExternalOutput" if name in taps else "Internal"
        return nc.dram_tensor(name, list(shape), dtype, kind=kind).ap()

    xin = dram_in("xT", (D, S))
    WqT = dram_in("WqT", (D, D)); WkT = dram_in("WkT", (D, D))
    WvT = dram_in("WvT", (D, D)); WoaT = dram_in("WoaT", (D, D))
    Wqkv = dram_in("Wqkv", (D, 3 * D)); Wo = dram_in("Wo", (D, D))
    Wr = dram_in("Wr", (D, D)); Wff1 = dram_in("Wff1", (D, FFD))
    Wff2 = dram_in("Wff2", (FFD, D)); Wg = dram_in("Wg", (D, NE))
    We1 = dram_in("We1", (NE, D, FFD)); We2 = dram_in("We2", (NE, FFD, D))
    Wout = dram_in("Wout", (D, D))
    boc = dram_in("boc", (128, 8)); uc = dram_in("uc", (128, 8))
    vc = dram_in("vc", (128, 8))
    lng = {i: dram_in(f"ln{i}g", (128, 8)) for i in (1, 2, 3)}
    lnb = {i: dram_in(f"ln{i}b", (128, 8)) for i in (1, 2, 3)}
    bff1c = dram_in("bff1c", (128, 32))
    bff2c = dram_in("bff2c", (128, 8))
    be1c = dram_in("be1c", (128, NE * 32))
    be2c = dram_in("be2c", (128, NE * 8))
    be2r = dram_in("be2r", (NE, D))
    c_pet = dram_in("c_pet", (D, S))
    c_cos = dram_in("c_cos", (128, S), BF16)
    c_sinm = dram_in("c_sinm", (128, S), BF16)
    c_id = dram_in("c_id", (128, 128))
    c_pswap = dram_in("c_pswap", (128, 128), BF16)
    c_eye8 = dram_in("c_eye8", (NE, NE * 128), BF16)

    yout = nc.dram_tensor("y", [D, S], F32, kind="ExternalOutput").ap()

    r_d = dram_int("r_d", (D, S), BF16)
    qu_d = dram_int("qu_d", (D, S), BF16)
    qv_d = dram_int("qv_d", (D, S), BF16)
    kk_d = dram_int("kk_d", (D, S), BF16)
    v_d = dram_int("v_d", (S, D), BF16)
    M_d = nc.dram_tensor("M_d", [2, 512, MW], F32, kind="Internal").ap()

    A = nc.alloc_sbuf_tensor
    a1 = [A(f"a1_{c}", [128, S], BF16) for c in range(NCH)]
    mid = [A(f"mid{i}", [128, 512], BF16) for i in range(32)]
    bdq = [A(f"bdq_{i}", [128, 512], F32) for i in range(8)]
    pacc = [A(f"pacc{i}", [128, 1024], BF16) for i in range(NE)]
    hid2 = [A(f"hid2_{i}", [128, 1024], BF16) for i in range(8)]
    hctx = [A(f"hctx{c}", [128, 512], BF16) for c in range(NCH)]
    wring = [A(f"wr{i}", [128, 1024], BF16) for i in range(6)]
    kr_b = [A(f"krb{i}", [128, S], BF16) for i in range(3)]
    xstage = [A(f"xst{i}", [128, 512], F32) for i in range(2)]
    cos_sb = A("cos_sb", [128, S], BF16)
    sinm_sb = A("sinm_sb", [128, S], BF16)
    voh_b = [A(f"voh{i}", [128, H * 65], BF16) for i in range(2)]
    mstage = [A(f"mst{i}", [128, 512], F32) for i in range(2)]
    lnrow = A("lnrow", [128, 2048], F32)
    wT_bf = A("wT_bf", [NE, S], BF16)
    quv_b = [A(f"quvb{i}", [128, 512], BF16) for i in range(3)]
    accst = [A(f"acc{i}", [128, 512], F32) for i in range(2)]
    ptile = [A(f"pt{i}", [128, 512], BF16) for i in range(3)]
    qstage = [A(f"qst{i}", [128, 512], BF16) for i in range(2)]
    rtmp = [A(f"rtmp{i}", [128, 512], BF16) for i in range(2)]
    ttmp = A("ttmp", [128, 512], F32)
    sqst = [A(f"sqst{i}", [128, 512], BF16) for i in range(2)]
    eye8_sb = A("eye8_sb", [NE, NE * 128], BF16)
    be2r_sb = A("be2r_sb", [NE, D], BF16)
    wbc_sb = A("wbc_sb", [128, 1024], BF16)
    recip_row = A("recip_row", [1, 512], F32)
    bcsb = A("bcsb", [64, 512], BF16)
    id_sb = A("id_sb", [128, 128], F32)
    ones_f = A("ones_f", [1, 128], F32)
    ones_bf = A("ones_bf", [1, 128], BF16)
    onesc_bf = A("onesc_bf", [128, 1], BF16)
    pswap_sb = A("pswap_sb", [128, 128], BF16)
    uvb_sb = A("uvb_sb", [128, 32], F32)
    lnp_sb = A("lnp_sb", [128, 48], F32)
    bff1_sb = A("bff1_sb", [128, 32], F32)
    be1_sb = A("be1_sb", [128, NE * 32], F32)
    be2_sb = A("be2_sb", [128, NE * 8], F32)
    wg_sb = A("wg_sb", [128, 64], BF16)
    rt = A("rt", [128, 64], F32)
    cvals = A("cvals", [128, 8], F32)

    pb = [nc.alloc_psum_tensor(f"pb{i}", [128, 512], F32) for i in range(8)]

    stck = ExitStack()
    sems = {}
    for name, step in (("s_pe", 1), ("s_act", 1), ("s_dve", 1), ("s_pool", 1),
                       ("d_w", 16), ("d_acc", 16), ("d_ld", 16), ("d_st", 16),
                       ("d_mw", 16), ("d_mr", 16)):
        sems[name] = Sem(stck.enter_context(nc.semaphore(name)), step)
    s_pe, s_act, s_dve, s_pool = (sems[k] for k in ("s_pe", "s_act", "s_dve", "s_pool"))
    d_w, d_acc, d_ld, d_st, d_mw, d_mr = (
        sems[k] for k in ("d_w", "d_acc", "d_ld", "d_st", "d_mw", "d_mr"))

    R = Rec()
    bank_free = [None] * 8
    bank_prod = [None] * 8

    def pe_begin_bank(i):
        if bank_free[i] is not None:
            R.wait("tensor", bank_free[i][0], bank_free[i][1])

    def pe_end_bank(i):
        bank_prod[i] = (s_pe, s_pe.n)

    def consume_bank(i, eng):
        R.wait(eng, bank_prod[i][0], bank_prod[i][1])

    def consumed_bank(i, sem):
        bank_free[i] = (sem, sem.n)

    def join(eng, *ss):
        for s in ss:
            R.wait(eng, s)

    ring_state = [{"loaded": 0, "lastuse": 0} for _ in wring]
    ring_ptr = [0]

    def wload_cols(dram, coloff, ncols, nrows, rowstride, rowoff=0):
        slot = ring_ptr[0]
        ring_ptr[0] = (slot + 1) % len(wring)
        stt = ring_state[slot]
        if stt["lastuse"]:
            R.wait("gpsimd", s_pe, stt["lastuse"])
        nfc = nrows // 128
        src = bass.AP(dram.tensor, rowoff * rowstride + coloff,
                      [[rowstride, 128], [128 * rowstride, nfc], [1, ncols]])
        R.op("gpsimd", lambda e, d=wring[slot][:, 0:nfc * ncols], s=src: e.dma_start(d, s), inc=d_w)
        stt["loaded"] = d_w.n
        return slot

    def ring_used(slot):
        ring_state[slot]["lastuse"] = s_pe.n

    for c in range(NCH):
        for tc in range(NTC):
            p = (c * NTC + tc) % 2
            R.op("sync", lambda e, c=c, tc=tc, p=p: e.dma_start(
                accst[p][:, :], xin[c * 128:(c + 1) * 128, tc * 512:(tc + 1) * 512]), inc=d_ld)
            join("sync", d_ld)
            R.op("sync", lambda e, c=c, tc=tc, p=p: e.dma_start(
                yout[c * 128:(c + 1) * 128, tc * 512:(tc + 1) * 512], accst[p][:, :]), inc=d_st)
    for dst, src in ((cos_sb, c_cos), (sinm_sb, c_sinm), (id_sb, c_id),
                     (pswap_sb, c_pswap), (bff1_sb, bff1c), (be1_sb, be1c),
                     (be2_sb, be2c), (eye8_sb, c_eye8)):
        R.op("sync", lambda e, d=dst, s=src: e.dma_start(d[:, :], s[:, :]), inc=d_ld)
    R.op("gpsimd", lambda e: e.dma_start(be2r_sb[:, :], be2r[:, :]), inc=d_w)
    wg_src = bass.AP(Wg.tensor, 0, [[NE, 128], [128 * NE, 8], [1, NE]])
    R.op("gpsimd", lambda e: e.dma_start(wg_sb[:, :], wg_src), inc=d_w)
    R.op("sync", lambda e: e.dma_start(uvb_sb[:, 0:8], uc[:, :]), inc=d_ld)
    R.op("sync", lambda e: e.dma_start(uvb_sb[:, 8:16], vc[:, :]), inc=d_ld)
    R.op("sync", lambda e: e.dma_start(uvb_sb[:, 16:24], boc[:, :]), inc=d_ld)
    R.op("sync", lambda e: e.dma_start(uvb_sb[:, 24:32], bff2c[:, :]), inc=d_ld)
    for i, k in ((0, 1), (2, 2), (4, 3)):
        R.op("sync", lambda e, i=i, k=k: e.dma_start(lnp_sb[:, i * 8:(i + 1) * 8], lng[k][:, :]), inc=d_ld)
        R.op("sync", lambda e, i=i, k=k: e.dma_start(lnp_sb[:, (i + 1) * 8:(i + 2) * 8], lnb[k][:, :]), inc=d_ld)
    join("gpsimd", d_ld)
    for i, v in ((0, EPS), (1, -0.5), (2, 0.125), (3, 1.0 / 64), (4, -1.0),
                 (5, 1.0), (6, 0.0)):
        R.op("gpsimd", lambda e, i=i, v=v: e.memset(cvals[:, i:i + 1], v))
    R.op("gpsimd", lambda e: e.memset(ones_f[:, :], 1.0))
    R.op("gpsimd", lambda e: e.memset(ones_bf[:, :], 1.0))
    R.op("gpsimd", lambda e: e.memset(onesc_bf[:, :], 1.0))
    R.op("gpsimd", lambda e: e.iota(rt[:, 48:56], [[1, NE]], channel_multiplier=0,
                                    allow_small_or_imprecise_dtypes=True))
    R.op("gpsimd", lambda e: e.tensor_scalar(rt[:, 56:64], rt[:, 48:56], -1.0, -16384.0, ALU.mult, ALU.add))
    R.op("gpsimd", lambda e: e.tensor_scalar(uvb_sb[:, 0:16], uvb_sb[:, 0:16], 0.125, None, ALU.mult))
    R.op("gpsimd", lambda e: e.tensor_scalar(lnp_sb[:, 32:48], lnp_sb[:, 32:48], 0.5, None, ALU.mult))
    for b in range(2):
        R.op("gpsimd", lambda e, b=b: e.memset(voh_b[b][:, :], 1.0))
    R.op("gpsimd", lambda e: e.memset(mstage[0][:, :], 0.0), inc=s_pool)
    join("sync", s_pool)
    for reg in range(2):
        for rb_ in range(4):
            dst = bass.AP(M_d.tensor, reg * 512 * MW + rb_ * 128 * MW + 2048,
                          [[MW, 128], [1, 512]])
            R.op("sync", lambda e, d=dst: e.dma_start(d, mstage[0][:, :]), inc=d_st)

    for fc in range(NCH):
        for jc in range(4):
            R.op("gpsimd", lambda e, fc=fc, jc=jc: e.dma_start(
                mid[fc * 4 + jc][:, :],
                c_pet[fc * 128:(fc + 1) * 128, jc * 512:(jc + 1) * 512]), inc=d_w)
    pe_loads = d_w.n
    for oc in range(NCH):
        slot = wload_cols(Wr, oc * 128, 128, D, D)
        for jc in range(4):
            bank = (oc * 4 + jc) % 2
            pe_begin_bank(bank)
            R.wait("tensor", d_w, ring_state[slot]["loaded"])
            R.wait("tensor", d_w, pe_loads)
            for fc in range(NCH):
                last = fc == NCH - 1
                R.op("tensor", lambda e, b=bank, s=slot, fc=fc, jc=jc, last=last: e.matmul(
                    pb[b][:, :], wring[s][:, fc * 128:(fc + 1) * 128],
                    mid[fc * 4 + jc][:, :], start=(fc == 0), stop=last),
                    inc=(s_pe if last else None))
            pe_end_bank(bank)
            ring_used(slot)
            consume_bank(bank, "scalar")
            join("scalar", d_st)
            q = (oc * 4 + jc) % 2
            R.op("scalar", lambda e, b=bank, q=q: e.copy(qstage[q][:, :], pb[b][:, :]), inc=s_act)
            consumed_bank(bank, s_act)
            join("sync", s_act)
            R.op("sync", lambda e, oc=oc, jc=jc, q=q: e.dma_start(
                bass.AP(r_d.tensor, oc * 128 * S + jc * 512, [[S, 128], [1, 512]]),
                qstage[q][:, :]), inc=d_st)

    def lr(cb):
        return lnrow[0:1, cb:cb + 512]

    LR_M, LR_T, LR_B, LR_RSTD = 0, 512, 1024, 1536

    def vsync():
        R.wait("vector", s_dve, s_dve.n)

    def ssync():
        R.wait("scalar", s_act, s_act.n)

    def ln_rows():
        consume_bank(6, "vector")
        consume_bank(7, "vector")
        R.op("vector", lambda e: e.tensor_scalar(lr(LR_M), pb[6][0:1, :], 1.0 / D, None, ALU.mult), inc=s_dve)
        vsync()
        R.op("vector", lambda e: e.tensor_mul(lr(LR_T), lr(LR_M), lr(LR_M)), inc=s_dve)
        vsync()
        R.op("vector", lambda e: e.scalar_tensor_tensor(
            lr(LR_T), pb[7][0:1, :], 1.0 / D, lr(LR_T), ALU.mult, ALU.subtract), inc=s_dve)
        consumed_bank(6, s_dve)
        consumed_bank(7, s_dve)
        join("scalar", s_dve)
        R.op("scalar", lambda e: e.activation(lr(LR_B), lr(LR_T), AF.Ln, bias=cvals[0:1, 0:1], scale=cvals[0:1, 5:6]), inc=s_act)
        ssync()
        R.op("scalar", lambda e: e.activation(lr(LR_RSTD), lr(LR_B), AF.Exp, bias=cvals[0:1, 6:7], scale=cvals[0:1, 1:2]), inc=s_act)
        join("vector", s_act)
        R.op("vector", lambda e: e.tensor_mul(lr(LR_B), lr(LR_M), lr(LR_RSTD)), inc=s_dve)
        return s_dve.n

    def ln_bcast(rv):
        R.wait("tensor", s_dve, rv)
        pe_begin_bank(6)
        pe_begin_bank(7)
        R.op("tensor", lambda e: e.matmul(pb[6][:, :], ones_f[0:1, :], lr(LR_RSTD), start=True, stop=True))
        R.op("tensor", lambda e: e.matmul(pb[7][:, :], ones_f[0:1, :], lr(LR_B), start=True, stop=True), inc=s_pe)
        pe_end_bank(6)
        pe_end_bank(7)

    sq_mark = {0: 0, 1: 0}

    def ln_stats_chunk(src_bf, c, sqslot):
        R.wait("scalar", s_pe, sq_mark[c % 2])
        R.op("scalar", lambda e, s=src_bf, q=sqslot: e.activation(q, s, AF.Square), inc=s_act)
        if c == 0:
            pe_begin_bank(6)
            pe_begin_bank(7)
        last = c == NCH - 1
        join("tensor", s_act)
        R.op("tensor", lambda e, s=src_bf, c=c, last=last: e.matmul(
            pb[6][0:1, :], onesc_bf[:, 0:1], s, start=(c == 0), stop=last,
            skip_group_check=True))
        R.op("tensor", lambda e, q=sqslot, c=c, last=last: e.matmul(
            pb[7][0:1, :], onesc_bf[:, 0:1], q, start=(c == 0), stop=last,
            skip_group_check=True), inc=s_pe)
        sq_mark[c % 2] = s_pe.n
        if last:
            pe_end_bank(6)
            pe_end_bank(7)

    def emit_ln0():
        for tc in range(NTC):
            cols = slice(tc * 512, (tc + 1) * 512)
            join("gpsimd", d_acc, s_pe, s_act)
            for c in range(NCH):
                R.op("gpsimd", lambda e, c=c, cols=cols: e.dma_start(
                    mid[c][:, :], yout[c * 128:(c + 1) * 128, cols]), inc=d_w)
            join("scalar", d_w)
            for c in range(NCH):
                ln_stats_chunk(mid[c][:, :], c, sqst[c % 2][:, :])
            rv = ln_rows()
            ln_bcast(rv)
            for c in range(NCH):
                p = c % 2
                join("sync", s_dve, d_acc)
                R.op("sync", lambda e, c=c, cols=cols, p=p: e.dma_start(
                    xstage[p][:, :], yout[c * 128:(c + 1) * 128, cols]), inc=d_ld)
                consume_bank(6, "vector")
                consume_bank(7, "vector")
                join("vector", d_ld)
                R.op("vector", lambda e, p=p: e.tensor_mul(ttmp[:, :], xstage[p][:, :], pb[6][:, :]))
                R.op("vector", lambda e, c=c, cols=cols: e.tensor_sub(
                    a1[c][:, cols], ttmp[:, :], pb[7][:, :]), inc=s_dve)
            consumed_bank(6, s_dve)
            consumed_bank(7, s_dve)

    def proj_group(bank, slot, src_chunks, cols):
        pe_begin_bank(bank)
        R.wait("tensor", d_w, ring_state[slot]["loaded"])
        join("tensor", s_dve, s_act)
        for fc in range(NCH):
            last = fc == NCH - 1
            R.op("tensor", lambda e, b=bank, s=slot, fc=fc, cols=cols, last=last,
                 src_chunks=src_chunks: e.matmul(
                pb[b][:, :], wring[s][:, fc * 128:(fc + 1) * 128],
                src_chunks[fc][:, cols], start=(fc == 0), stop=last),
                inc=(s_pe if last else None))
        pe_end_bank(bank)
        ring_used(slot)

    def emit_qkv_txl():
        for oc in range(16):
            slot = wload_cols(Wqkv, oc * 128, 128, D, 3 * D)
            for tc in range(NTC):
                bank = (oc * NTC + tc) % 2
                proj_group(bank, slot, a1, slice(tc * 512, (tc + 1) * 512))
                qs = (oc * NTC + tc) % 2
                if oc < 8:
                    for dram, boff in ((qu_d, 0), (qv_d, 8)):
                        consume_bank(bank, "scalar")
                        join("scalar", d_st)
                        R.op("scalar", lambda e, b=bank, qs=qs, oc=oc, boff=boff: e.activation(
                            qstage[qs][:, :], pb[b][:, :], AF.Identity,
                            bias=uvb_sb[:, boff + oc:boff + oc + 1],
                            scale=cvals[:, 2:3]), inc=s_act)
                        consumed_bank(bank, s_act)
                        join("sync", s_act)
                        R.op("sync", lambda e, d=dram, oc=oc, tc=tc, qs=qs: e.dma_start(
                            bass.AP(d.tensor, oc * 128 * S + tc * 512, [[S, 128], [1, 512]]),
                            qstage[qs][:, :]), inc=d_st)
                else:
                    consume_bank(bank, "scalar")
                    join("scalar", d_st)
                    R.op("scalar", lambda e, b=bank, qs=qs: e.copy(qstage[qs][:, :], pb[b][:, :]), inc=s_act)
                    consumed_bank(bank, s_act)
                    join("sync", s_act)
                    R.op("sync", lambda e, oc=oc, tc=tc, qs=qs: e.dma_start(
                        bass.AP(kk_d.tensor, (oc - 8) * 128 * S + tc * 512, [[S, 128], [1, 512]]),
                        qstage[qs][:, :]), inc=d_st)

    def emit_qkv_v(WsrcT=None):
        join("gpsimd", s_pe, s_act)
        for vc in range(2):
            base = 16 if vc == 0 else 24
            for fc in range(NCH):
                if WsrcT is None:
                    src = Wqkv[fc * 128:(fc + 1) * 128, 2 * D + vc * 512:2 * D + (vc + 1) * 512]
                else:
                    src = WsrcT[fc * 128:(fc + 1) * 128, vc * 512:(vc + 1) * 512]
                R.op("gpsimd", lambda e, fc=fc, base=base, src=src: e.dma_start(
                    mid[base + fc][:, :], src), inc=d_w)
        vw = d_w.n
        for tt in range(16):
            for vc in range(2):
                base = 16 if vc == 0 else 24
                bank = (tt * 2 + vc) % 2
                pe_begin_bank(bank)
                R.wait("tensor", d_w, vw)
                join("tensor", s_dve)
                for fc in range(NCH):
                    last = fc == NCH - 1
                    R.op("tensor", lambda e, b=bank, fc=fc, tt=tt, base=base, last=last: e.matmul(
                        pb[b][:, :], a1[fc][:, tt * 128:(tt + 1) * 128],
                        mid[base + fc][:, :], start=(fc == 0), stop=last),
                        inc=(s_pe if last else None))
                pe_end_bank(bank)
                qs = (tt * 2 + vc) % 2
                consume_bank(bank, "scalar")
                join("scalar", d_st)
                R.op("scalar", lambda e, b=bank, qs=qs: e.copy(qstage[qs][:, :], pb[b][:, :]), inc=s_act)
                consumed_bank(bank, s_act)
                join("sync", s_act)
                R.op("sync", lambda e, tt=tt, vc=vc, qs=qs: e.dma_start(
                    bass.AP(v_d.tensor, tt * 128 * D + vc * 512, [[D, 128], [1, 512]]),
                    qstage[qs][:, :]), inc=d_st)

    def emit_qkv_rope():
        for WT, dram, scl in ((WqT, qu_d, "q"), (WkT, kk_d, "k")):
            for oc in range(8):
                slot = wload_cols(WT, oc * 128, 128, D, D)
                for tc in range(NTC):
                    bank = (oc * NTC + tc) % 2
                    proj_group(bank, slot, a1, slice(tc * 512, (tc + 1) * 512))
                    qs = (oc * NTC + tc) % 2
                    consume_bank(bank, "scalar")
                    join("scalar", s_pe, d_st)
                    scl_ap = cvals[:, 3:4] if scl == "q" else cvals[:, 5:6]
                    R.op("scalar", lambda e, b=bank, qs=qs, sa=scl_ap: e.activation(
                        qstage[qs][:, :], pb[b][:, :], AF.Identity, bias=cvals[:, 6:7], scale=sa), inc=s_act)
                    pe_begin_bank(4)
                    join("tensor", s_act)
                    R.op("tensor", lambda e, qs=qs: e.matmul(
                        pb[4][:, :], pswap_sb[:, :], qstage[qs][:, :], start=True, stop=True), inc=s_pe)
                    pe_end_bank(4)
                    consumed_bank(bank, s_pe)
                    consume_bank(4, "vector")
                    tcs = slice(tc * 512, (tc + 1) * 512)
                    join("vector", d_st)
                    R.op("vector", lambda e, tcs=tcs: e.tensor_mul(rtmp[0][:, :], pb[4][:, :], sinm_sb[:, tcs]))
                    R.op("vector", lambda e, qs=qs, tcs=tcs: e.tensor_mul(rtmp[1][:, :], qstage[qs][:, :], cos_sb[:, tcs]))
                    R.op("vector", lambda e: e.tensor_add(rtmp[0][:, :], rtmp[0][:, :], rtmp[1][:, :]), inc=s_dve)
                    consumed_bank(4, s_dve)
                    join("sync", s_dve)
                    R.op("sync", lambda e, d=dram, oc=oc, tc=tc: e.dma_start(
                        bass.AP(d.tensor, oc * 128 * S + tc * 512, [[S, 128], [1, 512]]),
                        rtmp[0][:, :]), inc=d_st)

    def emit_attention(txl, epilogue):
        head_guard = {0: 0, 1: 0, 2: 0}
        voh_guard = {0: 0, 1: 0}
        bdq_free = {0: s_pe.n, 1: s_pe.n}
        reg_rb_mark = {0: 0, 1: 0}
        rb_mark = {}
        hctx_guard = [s_pe.n]
        ctx_guard = [s_pe.n]
        join("sync", d_st)

        def nk_of(qc):
            return 4 * (qc + 1)

        def head_load(it):
            qc, h = it // 16, it % 16
            kb = it % 3
            R.wait("sync", s_pe, head_guard[kb])
            R.op("sync", lambda e, h=h, kb=kb: e.dma_start(
                kr_b[kb][0:64, :], kk_d[h * 64:(h + 1) * 64, :]), inc=d_ld)
            if txl:
                R.op("sync", lambda e, h=h, kb=kb: e.dma_start(
                    kr_b[kb][64:128, :], r_d[h * 64:(h + 1) * 64, :]), inc=d_ld)
            R.op("sync", lambda e, h=h, kb=kb, qc=qc: e.dma_start(
                quv_b[kb][0:64, :], qu_d[h * 64:(h + 1) * 64, qc * 512:(qc + 1) * 512]), inc=d_ld)
            if txl:
                R.op("sync", lambda e, h=h, kb=kb, qc=qc: e.dma_start(
                    quv_b[kb][64:128, :], qv_d[h * 64:(h + 1) * 64, qc * 512:(qc + 1) * 512]), inc=d_ld)

        def voh_load(it):
            qc, h = it // 16, it % 16
            vb = it % 2
            R.wait("sync", s_pe, voh_guard[vb])
            for g4 in range(4):
                src = bass.AP(v_d.tensor, (g4 * 4) * 128 * D + h * 64,
                              [[D, 128], [128 * D, 4], [1, 64]])
                dst = bass.AP(voh_b[vb].ap().tensor, (g4 * 4) * 65,
                              [[H * 65, 128], [65, 4], [1, 64]])
                R.op("sync", lambda e, d=dst, s=src: e.dma_start(d, s), inc=d_ld)

        def emit_M(it):
            qc, h = it // 16, it % 16
            reg = it % 2
            kb = it % 3
            jlo = 1536 - 512 * qc
            njc = qc + 1
            R.wait("tensor", d_ld, head_ld_mark[it])
            R.wait("sync", d_mr, reg_rb_mark[reg])
            for qs in range(4):
                for jc in range(njc):
                    bank = 3 + (qs * njc + jc) % 2
                    pe_begin_bank(bank)
                    R.op("tensor", lambda e, b=bank, kb=kb, qs=qs, jl=jlo + jc * 512: e.matmul(
                        pb[b][:, :], quv_b[kb][64:128, qs * 128:(qs + 1) * 128],
                        kr_b[kb][64:128, jl:jl + 512], start=True, stop=True), inc=s_pe)
                    pe_end_bank(bank)
                    ms = (qs * njc + jc) % 2
                    consume_bank(bank, "scalar")
                    join("scalar", d_mw)
                    R.op("scalar", lambda e, b=bank, ms=ms: e.copy(mstage[ms][:, :], pb[b][:, :]), inc=s_act)
                    consumed_bank(bank, s_act)
                    join("sync", s_act)
                    dst = bass.AP(M_d.tensor, reg * 512 * MW + qs * 128 * MW + jlo + jc * 512,
                                  [[MW, 128], [1, 512]])
                    R.op("sync", lambda e, d=dst, ms=ms: e.dma_start(d, mstage[ms][:, :]), inc=d_mw)

        def issue_rb(it, g):
            qc = it // 16
            reg = it % 2
            R.wait("sync", s_pe, bdq_free[g % 2])
            for qs in range(4):
                off = reg * 512 * MW + qs * 128 * MW + (2047 - qc * 512 - qs * 128 + 512 * g)
                src = bass.AP(M_d.tensor, off, [[MW - 1, 128], [1, 512]])
                R.op("sync", lambda e, s=src, g=g, qs=qs: e.dma_start(
                    bdq[(g % 2) * 4 + qs][:, :], s), inc=d_mr)
            rb_mark[g] = d_mr.n

        def emit_head(it):
            qc, h = it // 16, it % 16
            nk = nk_of(qc)
            kb = it % 3
            vb = it % 2
            R.wait("tensor", d_ld, head_ld_mark[it])
            R.wait("tensor", d_ld, voh_ld_mark[it])
            for kt in range(nk):
                g = kt // 4
                stb = kt % 2
                if txl and kt % 4 == 0 and g + 1 <= (nk - 1) // 4:
                    issue_rb(it, g + 1)
                pe_begin_bank(stb)
                R.op("tensor", lambda e, b=stb, kb=kb, kt=kt, st=(not txl): e.matmul(
                    pb[b][:, :], kr_b[kb][0:64, kt * 128:(kt + 1) * 128],
                    quv_b[kb][0:64, :], start=True, stop=(not txl),
                    skip_group_check=True), inc=(None if txl else s_pe))
                if txl:
                    R.wait("tensor", d_mr, rb_mark[g])
                    kl = (kt % 4) * 128
                    for qs in range(4):
                        R.op("tensor", lambda e, b=stb, g=g, qs=qs, kl=kl: e.matmul(
                            pb[b][:, qs * 128:(qs + 1) * 128],
                            bdq[(g % 2) * 4 + qs][:, kl:kl + 128],
                            id_sb[:, :], start=False, stop=(qs == 3),
                            is_transpose=True, skip_group_check=True),
                            inc=(s_pe if qs == 3 else None))
                pe_end_bank(stb)
                if txl:
                    bdq_free[g % 2] = s_pe.n
                consume_bank(stb, "scalar")
                join("scalar", s_pool)
                R.op("scalar", lambda e, b=stb, p=kt % 2: e.activation(
                    ptile[p][:, :], pb[b][:, :], AF.Exp), inc=s_act)
                consumed_bank(stb, s_act)
                use_p = kt % 2
                if kt >= nk - 4:
                    delta = kt * 128 - qc * 512
                    join("gpsimd", s_act)
                    R.wait("gpsimd", s_pe, ctx_guard[0])
                    R.op("gpsimd", lambda e, p=kt % 2, delta=delta: e.affine_select(
                        ptile[2][:, :], ptile[p][:, :], [[1, 512]], ALU.is_ge,
                        0.0, base=-delta, channel_multiplier=-1), inc=s_pool)
                    use_p = 2
                pe_begin_bank(2)
                if use_p == 2:
                    join("tensor", s_pool)
                else:
                    join("tensor", s_act)
                R.op("tensor", lambda e, vb=vb, h=h, p=use_p, kt=kt, nk=nk: e.matmul(
                    pb[2][0:65, :], voh_b[vb][:, kt * 65:(kt + 1) * 65],
                    ptile[p][:, :], start=(kt == 0), stop=(kt == nk - 1),
                    skip_group_check=True), inc=s_pe)
                if use_p == 2:
                    ctx_guard[0] = s_pe.n
                pe_end_bank(2)
            consume_bank(2, "vector")
            R.op("vector", lambda e: e.reciprocal(recip_row[:, :], pb[2][64:65, :]), inc=s_dve)
            pe_begin_bank(5)
            R.wait("tensor", s_dve, s_dve.n)
            R.op("tensor", lambda e: e.matmul(pb[5][0:64, :], ones_f[0:1, 0:64],
                                              recip_row[0:1, :], start=True, stop=True), inc=s_pe)
            pe_end_bank(5)
            consume_bank(5, "scalar")
            join("scalar", s_dve)
            R.op("scalar", lambda e: e.copy(bcsb[:, :], pb[5][0:64, :]), inc=s_act)
            consumed_bank(5, s_act)
            ch, half = h // 2, (h % 2) * 64
            join("vector", s_act)
            R.wait("vector", s_pe, hctx_guard[0])
            R.op("vector", lambda e, ch=ch, half=half: e.tensor_mul(
                hctx[ch][half:half + 64, :], pb[2][0:64, :], bcsb[:, :]), inc=s_dve)
            consumed_bank(2, s_dve)

        head_ld_mark = {}
        voh_ld_mark = {}

        head_load(0); head_ld_mark[0] = d_ld.n
        voh_load(0); voh_ld_mark[0] = d_ld.n
        if txl:
            emit_M(0)
            join("sync", d_mw)
            issue_rb(0, 0)
        head_load(1); head_ld_mark[1] = d_ld.n
        voh_load(1); voh_ld_mark[1] = d_ld.n

        for it in range(64):
            qc, h = it // 16, it % 16
            if txl and it + 1 < 64:
                emit_M(it + 1)
            if it + 2 < 64:
                head_load(it + 2); head_ld_mark[it + 2] = d_ld.n
            emit_head(it)
            head_guard[it % 3] = s_pe.n
            voh_guard[it % 2] = s_pe.n
            if it + 2 < 64:
                voh_load(it + 2); voh_ld_mark[it + 2] = d_ld.n
            if txl:
                reg_rb_mark[it % 2] = d_mr.n
            if txl and it + 1 < 64:
                join("sync", d_mw)
                issue_rb(it + 1, 0)
            if h == 15:
                epilogue(qc)
                hctx_guard[0] = s_pe.n

    def ffn_epilogue(qc):
        cols = slice(qc * 512, (qc + 1) * 512)
        for oc in range(NCH):
            slot = wload_cols(Wo, oc * 128, 128, D, D)
            bank = oc % 2
            pe_begin_bank(bank)
            R.wait("tensor", d_w, ring_state[slot]["loaded"])
            join("tensor", s_dve)
            for dc in range(NCH):
                last = dc == NCH - 1
                R.op("tensor", lambda e, b=bank, s=slot, dc=dc, last=last: e.matmul(
                    pb[b][:, :], wring[s][:, dc * 128:(dc + 1) * 128],
                    hctx[dc][:, :], start=(dc == 0), stop=last),
                    inc=(s_pe if last else None))
            pe_end_bank(bank)
            ring_used(slot)
            consume_bank(bank, "vector")
            R.op("vector", lambda e, b=bank, oc=oc, cols=cols: e.tensor_add(
                a1[oc][:, cols], pb[b][:, :], a1[oc][:, cols]), inc=s_dve)
            consumed_bank(bank, s_dve)
        emit_ln_inplace(cols, 0, 8)

    def emit_ln_inplace(cols, gcol, bcol):
        join("scalar", s_dve)
        for c in range(NCH):
            ln_stats_chunk(a1[c][:, cols], c, sqst[c % 2][:, :])
        rv = ln_rows()
        ln_bcast(rv)
        for c in range(NCH):
            consume_bank(6, "vector")
            consume_bank(7, "vector")
            join("vector", s_act)
            R.op("vector", lambda e, c=c, cols=cols: e.tensor_mul(ttmp[:, :], a1[c][:, cols], pb[6][:, :]))
            R.op("vector", lambda e, c=c: e.tensor_sub(xstage[0][:, :], ttmp[:, :], pb[7][:, :]), inc=s_dve)
            join("scalar", s_dve)
            R.op("scalar", lambda e, c=c, cols=cols, g=gcol, b=bcol: e.activation(
                a1[c][:, cols], xstage[0][:, :], AF.Identity,
                bias=lnp_sb[:, b + c:b + c + 1], scale=lnp_sb[:, g + c:g + c + 1]), inc=s_act)
        consumed_bank(6, s_dve)
        consumed_bank(7, s_dve)

    def attn_epilogue(qc):
        cols = slice(qc * 512, (qc + 1) * 512)
        for oc in range(NCH):
            slot = wload_cols(WoaT, oc * 128, 128, D, D)
            bank = oc % 2
            pe_begin_bank(bank)
            R.wait("tensor", d_w, ring_state[slot]["loaded"])
            join("tensor", s_dve)
            for dc in range(NCH):
                last = dc == NCH - 1
                R.op("tensor", lambda e, b=bank, s=slot, dc=dc, last=last: e.matmul(
                    pb[b][:, :], wring[s][:, dc * 128:(dc + 1) * 128],
                    hctx[dc][:, :], start=(dc == 0), stop=last),
                    inc=(s_pe if last else None))
            pe_end_bank(bank)
            ring_used(slot)
            consume_bank(bank, "scalar")
            join("scalar", d_acc)
            ap_ = oc % 2
            R.op("scalar", lambda e, b=bank, oc=oc, ap_=ap_: e.activation(
                accst[ap_][:, :], pb[b][:, :], AF.Identity,
                bias=uvb_sb[:, 16 + oc:16 + oc + 1], scale=cvals[:, 5:6]), inc=s_act)
            consumed_bank(bank, s_act)
            join("gpsimd", s_act)
            R.op("gpsimd", lambda e, oc=oc, cols=cols, ap_=ap_: e.dma_start(
                yout[oc * 128:(oc + 1) * 128, cols], accst[ap_][:, :],
                accum_op=ALU.add), inc=d_acc)

    def ff1v(oc, ts):
        if oc < 16:
            return mid[oc * 2 + ts][:, :]
        if oc < 24:
            t = bdq[oc - 16].ap().bitcast(BF16)
            return t[:, ts * 512:(ts + 1) * 512]
        return hid2[oc - 24][:, ts * 512:(ts + 1) * 512]

    def emit_ff():
        for th in range(2):
            base = th * 1024
            for oc in range(32):
                slot = wload_cols(Wff1, oc * 128, 128, D, FFD)
                for ts in range(2):
                    cols = slice(base + ts * 512, base + (ts + 1) * 512)
                    bank = (oc * 2 + ts) % 2
                    proj_group(bank, slot, a1, cols)
                    consume_bank(bank, "scalar")
                    join("scalar", s_pe)
                    R.op("scalar", lambda e, b=bank, oc=oc, ts=ts: e.activation(
                        ff1v(oc, ts), pb[b][:, :], AF.Relu,
                        bias=bff1_sb[:, oc:oc + 1], scale=cvals[:, 5:6]), inc=s_act)
                    consumed_bank(bank, s_act)
            for dc in range(NCH):
                slots = [wload_cols(Wff2, dc * 128, 128, 1024, D, rowoff=rg * 1024)
                         for rg in range(4)]
                for ts in range(2):
                    cols = slice(base + ts * 512, base + (ts + 1) * 512)
                    bank = 2 + ts
                    pe_begin_bank(bank)
                    for rg in range(4):
                        R.wait("tensor", d_w, ring_state[slots[rg]]["loaded"])
                    join("tensor", s_act)
                    for oc in range(32):
                        last = oc == 31
                        sl = slots[oc // 8]
                        R.op("tensor", lambda e, b=bank, sl=sl, oc=oc, ts=ts, last=last: e.matmul(
                            pb[b][:, :], wring[sl][:, (oc % 8) * 128:(oc % 8 + 1) * 128],
                            ff1v(oc, ts), start=(oc == 0), stop=last),
                            inc=(s_pe if last else None))
                    pe_end_bank(bank)
                    for rg in range(4):
                        ring_used(slots[rg])
                    consume_bank(bank, "scalar")
                    join("scalar", s_dve)
                    qp = ts
                    R.op("scalar", lambda e, b=bank, dc=dc, qp=qp: e.activation(
                        qstage[qp][:, :], pb[b][:, :], AF.Identity,
                        bias=uvb_sb[:, 24 + dc:24 + dc + 1], scale=cvals[:, 5:6]), inc=s_act)
                    consumed_bank(bank, s_act)
                    join("vector", s_act)
                    R.op("vector", lambda e, dc=dc, cols=cols, qp=qp: e.tensor_add(
                        a1[dc][:, cols], qstage[qp][:, :], a1[dc][:, cols]), inc=s_dve)
            for ts in range(2):
                emit_ln_inplace(slice(base + ts * 512, base + (ts + 1) * 512), 16, 24)

    rt_pe_mark = [0]

    def emit_routing():
        t_rt = None
        if "t_rt" in taps:
            t_rt = nc.dram_tensor("t_rt", [128, 256], F32, kind="ExternalOutput").ap()
        for tt in range(16):
            bank = tt % 2
            pe_begin_bank(bank)
            join("tensor", s_dve, s_act)
            for fc in range(NCH):
                last = fc == NCH - 1
                R.op("tensor", lambda e, b=bank, fc=fc, tt=tt, last=last: e.matmul(
                    pb[b][0:128, 0:8], a1[fc][:, tt * 128:(tt + 1) * 128],
                    wg_sb[:, fc * 8:(fc + 1) * 8], start=(fc == 0), stop=last,
                    skip_group_check=True), inc=(s_pe if last else None))
            pe_end_bank(bank)
            consume_bank(bank, "vector")
            join("vector", s_act, d_st)
            R.wait("vector", s_pe, rt_pe_mark[0])
            R.op("vector", lambda e, b=bank: e.tensor_copy(rt[:, 0:8], pb[b][:, 0:8]), inc=s_dve)
            consumed_bank(bank, s_dve)
            vsync()
            R.op("vector", lambda e: e.reduce_max(rt[:, 8:9], rt[:, 0:8], axis=AX.X, negate=True), inc=s_dve)
            vsync()
            R.op("vector", lambda e: e.reduce_max(rt[:, 13:14], rt[:, 0:8], axis=AX.X), inc=s_dve)
            join("scalar", s_dve)
            R.op("scalar", lambda e: e.activation(rt[:, 16:24], rt[:, 0:8], AF.Exp,
                                                  bias=rt[:, 8:9], scale=cvals[:, 5:6]), inc=s_act)
            ssync()
            R.op("scalar", lambda e: e.activation(rt[:, 24:32], rt[:, 0:8], AF.Identity,
                                                  bias=rt[:, 13:14], scale=cvals[:, 6:7]), inc=s_act)
            join("vector", s_act)
            R.op("vector", lambda e: e.tensor_tensor(rt[:, 32:40], rt[:, 0:8], rt[:, 24:32], ALU.is_equal), inc=s_dve)
            vsync()
            R.op("vector", lambda e: e.reduce_sum(rt[:, 9:10], rt[:, 16:24], axis=AX.X), inc=s_dve)
            vsync()
            R.op("vector", lambda e: e.reciprocal(rt[:, 10:11], rt[:, 9:10]), inc=s_dve)
            vsync()
            R.op("vector", lambda e: e.scalar_tensor_tensor(
                rt[:, 40:48], rt[:, 32:40], 16384.0, rt[:, 56:64], ALU.mult, ALU.add), inc=s_dve)
            vsync()
            R.op("vector", lambda e: e.reduce_max(rt[:, 11:12], rt[:, 40:48], axis=AX.X), inc=s_dve)
            join("scalar", s_dve)
            R.op("scalar", lambda e: e.activation(rt[:, 40:48], rt[:, 48:56], AF.Square,
                                                  bias=rt[:, 11:12], scale=cvals[:, 5:6]), inc=s_act)
            ssync()
            R.op("scalar", lambda e: e.activation(rt[:, 32:40], rt[:, 40:48], AF.Relu,
                                                  bias=cvals[:, 5:6], scale=cvals[:, 4:5]), inc=s_act)
            ssync()
            R.op("scalar", lambda e: e.activation(rt[:, 24:32], rt[:, 32:40], AF.Identity,
                                                  bias=cvals[:, 6:7], scale=rt[:, 10:11]), inc=s_act)
            if t_rt is not None:
                join("sync", s_act, s_dve)
                R.op("sync", lambda e, tt=tt, t_rt=t_rt: e.dma_start(t_rt[:, tt * 16:(tt + 1) * 16], rt[:, 0:16]), inc=d_st)
            pe_begin_bank(5)
            join("tensor", s_act)
            R.op("tensor", lambda e: e.matmul(pb[5][0:8, 0:128], rt[:, 24:32], id_sb[:, :],
                                              start=True, stop=True, is_transpose=True), inc=s_pe)
            pe_end_bank(5)
            rt_pe_mark[0] = s_pe.n
            consume_bank(5, "scalar")
            join("scalar", s_dve)
            R.op("scalar", lambda e, tt=tt: e.copy(wT_bf[0:8, tt * 128:(tt + 1) * 128], pb[5][0:8, 0:128]), inc=s_act)
            consumed_bank(5, s_act)

    def emit_moe():
        first_th = True
        for th in range(2):
            base = th * 1024
            for e_ in range(NE):
                for oc in range(32):
                    slot = wload_cols(We1, e_ * D * FFD + oc * 128, 128, D, FFD)
                    for ts in range(2):
                        cols = slice(base + ts * 512, base + (ts + 1) * 512)
                        bank = (oc * 2 + ts) % 2
                        proj_group(bank, slot, a1, cols)
                        consume_bank(bank, "scalar")
                        join("scalar", s_pe)
                        R.op("scalar", lambda e, b=bank, e_=e_, oc=oc, ts=ts: e.activation(
                            ff1v(oc, ts), pb[b][:, :], AF.Relu,
                            bias=be1_sb[:, e_ * 32 + oc:e_ * 32 + oc + 1], scale=cvals[:, 5:6]), inc=s_act)
                        consumed_bank(bank, s_act)
                for ts in range(2):
                    pe_begin_bank(5)
                    join("tensor", s_act)
                    R.op("tensor", lambda e, e_=e_, base=base, ts=ts: e.matmul(
                        pb[5][:, :], eye8_sb[:, e_ * 128:(e_ + 1) * 128],
                        wT_bf[0:8, base + ts * 512:base + (ts + 1) * 512],
                        start=True, stop=True), inc=s_pe)
                    pe_end_bank(5)
                    consume_bank(5, "scalar")
                    join("scalar", s_dve)
                    R.op("scalar", lambda e, ts=ts: e.copy(wbc_sb[:, ts * 512:(ts + 1) * 512], pb[5][:, :]), inc=s_act)
                    consumed_bank(5, s_act)
                if first_th and e_ == 0 and "t_wbc" in taps:
                    t_wbc = nc.dram_tensor("t_wbc", [128, 1024], BF16, kind="ExternalOutput").ap()
                    join("sync", s_act)
                    R.op("sync", lambda e, t=t_wbc: e.dma_start(t[:, :], wbc_sb[:, :]), inc=d_st)
                    join("gpsimd", d_st)
                for dc in range(NCH):
                    slots = [wload_cols(We2, e_ * FFD * D + dc * 128, 128, 1024, D,
                                        rowoff=rg * 1024) for rg in range(4)]
                    for ts in range(2):
                        bank = 2 + ts
                        pe_begin_bank(bank)
                        for rg in range(4):
                            R.wait("tensor", d_w, ring_state[slots[rg]]["loaded"])
                        join("tensor", s_act)
                        for oc in range(32):
                            last = oc == 31
                            sl = slots[oc // 8]
                            R.op("tensor", lambda e, b=bank, sl=sl, oc=oc, ts=ts, last=last: e.matmul(
                                pb[b][:, :], wring[sl][:, (oc % 8) * 128:(oc % 8 + 1) * 128],
                                ff1v(oc, ts), start=(oc == 0), stop=last),
                                inc=(s_pe if last else None))
                        pe_end_bank(bank)
                        for rg in range(4):
                            ring_used(slots[rg])
                        consume_bank(bank, "scalar")
                        join("scalar", s_dve)
                        qp = ts
                        R.op("scalar", lambda e, b=bank, dc=dc, e_=e_, qp=qp: e.activation(
                            qstage[qp][:, :], pb[b][:, :], AF.Identity,
                            bias=be2_sb[:, e_ * 8 + dc:e_ * 8 + dc + 1], scale=cvals[:, 5:6]), inc=s_act)
                        consumed_bank(bank, s_act)
                        join("vector", s_act)
                        pc = slice(ts * 512, (ts + 1) * 512)
                        if first_th and e_ == 0 and dc == 0 and ts == 0 and "t_wbc" in taps:
                            t_qs = nc.dram_tensor("t_qs", [128, 512], BF16, kind="ExternalOutput").ap()
                            join("sync", s_act)
                            R.op("sync", lambda e, t=t_qs, qp=qp: e.dma_start(t[:, :], qstage[qp][:, :]), inc=d_st)
                            join("vector", d_st)
                        if e_ == 0:
                            R.op("vector", lambda e, dc=dc, pc=pc, ts=ts, qp=qp: e.tensor_mul(
                                pacc[dc][:, pc], qstage[qp][:, :],
                                wbc_sb[:, ts * 512:(ts + 1) * 512]), inc=s_dve)
                        else:
                            R.op("vector", lambda e, ts=ts, qp=qp: e.tensor_mul(
                                rtmp[0][:, :], qstage[qp][:, :],
                                wbc_sb[:, ts * 512:(ts + 1) * 512]))
                            R.op("vector", lambda e, dc=dc, pc=pc: e.tensor_add(
                                pacc[dc][:, pc], pacc[dc][:, pc], rtmp[0][:, :]), inc=s_dve)
            for ts in range(2):
                cols = slice(base + ts * 512, base + (ts + 1) * 512)
                pc = slice(ts * 512, (ts + 1) * 512)
                for dc in range(NCH):
                    R.op("vector", lambda e, dc=dc, pc=pc, cols=cols: e.tensor_add(
                        pacc[dc][:, pc], pacc[dc][:, pc], a1[dc][:, cols]), inc=s_dve)
                join("scalar", s_dve)
                for c in range(NCH):
                    ln_stats_chunk(pacc[c][:, pc], c, sqst[c % 2][:, :])
                rv = ln_rows()
                ln_bcast(rv)
                for c in range(NCH):
                    consume_bank(6, "vector")
                    consume_bank(7, "vector")
                    join("vector", s_act)
                    R.op("vector", lambda e, c=c, pc=pc: e.tensor_mul(ttmp[:, :], pacc[c][:, pc], pb[6][:, :]))
                    R.op("vector", lambda e: e.tensor_sub(xstage[0][:, :], ttmp[:, :], pb[7][:, :]), inc=s_dve)
                    join("scalar", s_dve)
                    R.op("scalar", lambda e, c=c, cols=cols: e.activation(
                        a1[c][:, cols], xstage[0][:, :], AF.Identity,
                        bias=lnp_sb[:, 40 + c:40 + c + 1], scale=lnp_sb[:, 32 + c:32 + c + 1]), inc=s_act)
                consumed_bank(6, s_dve)
                consumed_bank(7, s_dve)
                for oc in range(NCH):
                    slot = wload_cols(Wout, oc * 128, 128, D, D)
                    bank = oc % 2
                    proj_group(bank, slot, a1, cols)
                    consume_bank(bank, "scalar")
                    join("scalar", d_acc)
                    ap_ = oc % 2
                    R.op("scalar", lambda e, b=bank, ap_=ap_: e.copy(accst[ap_][:, :], pb[b][:, :]), inc=s_act)
                    consumed_bank(bank, s_act)
                    join("gpsimd", s_act)
                    R.op("gpsimd", lambda e, oc=oc, cols=cols, ap_=ap_: e.dma_start(
                        yout[oc * 128:(oc + 1) * 128, cols], accst[ap_][:, :],
                        accum_op=ALU.add), inc=d_acc)
            if first_th and "t_p3a" in taps:
                tap_pacc("t_p3a")
            first_th = False

    def tap_a1(name):
        t = nc.dram_tensor(name, [D, S], F32, kind="ExternalOutput").ap()
        join("vector", s_act, s_pe)
        for c in range(NCH):
            for tc in range(NTC):
                join("vector", d_st)
                R.op("vector", lambda e, c=c, tc=tc: e.tensor_copy(
                    ttmp[:, :], a1[c][:, tc * 512:(tc + 1) * 512]), inc=s_dve)
                join("sync", s_dve)
                R.op("sync", lambda e, t=t, c=c, tc=tc: e.dma_start(
                    bass.AP(t.tensor, c * 128 * S + tc * 512, [[S, 128], [1, 512]]),
                    ttmp[:, :]), inc=d_st)

    def tap_y(name):
        t = nc.dram_tensor(name, [D, S], F32, kind="ExternalOutput").ap()
        join("sync", d_acc, s_act, s_dve)
        for c in range(NCH):
            for tc in range(NTC):
                p = (c + tc) % 2
                R.op("sync", lambda e, c=c, tc=tc, p=p: e.dma_start(
                    xstage[p][:, :], yout[c * 128:(c + 1) * 128, tc * 512:(tc + 1) * 512]), inc=d_ld)
                join("sync", d_ld)
                R.op("sync", lambda e, t=t, c=c, tc=tc, p=p: e.dma_start(
                    bass.AP(t.tensor, c * 128 * S + tc * 512, [[S, 128], [1, 512]]),
                    xstage[p][:, :]), inc=d_st)

    def tap_hctx(name):
        t = nc.dram_tensor(name, [D, 512], F32, kind="ExternalOutput").ap()
        join("vector", s_act, s_pe, s_dve)
        for c in range(NCH):
            join("vector", d_st)
            R.op("vector", lambda e, c=c: e.tensor_copy(ttmp[:, :], hctx[c][:, :]), inc=s_dve)
            join("sync", s_dve)
            R.op("sync", lambda e, t=t, c=c: e.dma_start(
                bass.AP(t.tensor, c * 128 * 512, [[512, 128], [1, 512]]), ttmp[:, :]), inc=d_st)

    def tap_wt(name):
        t = nc.dram_tensor(name, [NE, S], BF16, kind="ExternalOutput").ap()
        join("sync", s_act, s_dve)
        R.op("sync", lambda e, t=t: e.dma_start(t[:, :], wT_bf[:, :]), inc=d_st)

    def tap_pacc(name):
        t = nc.dram_tensor(name, [D, 1024], BF16, kind="ExternalOutput").ap()
        join("sync", s_dve, s_act)
        for c in range(NCH):
            R.op("sync", lambda e, t=t, c=c: e.dma_start(
                t[c * 128:(c + 1) * 128, :], pacc[c][:, :]), inc=d_st)

    join("tensor", s_pool)

    def ffn_call(first):
        emit_ln0()
        if first and "t_h" in taps:
            tap_a1("t_h")
        emit_qkv_txl()
        emit_qkv_v()
        emit_attention(True, ffn_epilogue)
        if first and "t_ctx" in taps:
            tap_hctx("t_ctx")
        if first and "t_h1" in taps:
            tap_a1("t_h1")
        emit_ff()
        if first and "t_h2" in taps:
            tap_a1("t_h2")
        emit_routing()
        if first and "t_wt" in taps:
            tap_wt("t_wt")
        emit_moe()
        if first and "t_p3" in taps:
            tap_pacc("t_p3")
        if first and "t_h3" in taps:
            tap_a1("t_h3")

    def attn_call():
        emit_ln0()
        emit_qkv_rope()
        emit_qkv_v(WsrcT=WvT)
        emit_attention(False, attn_epilogue)

    ffn_call(True)
    if "t_x1" in taps:
        tap_y("t_x1")
    if stages >= 2:
        attn_call()
        if "t_x2" in taps:
            tap_y("t_x2")
    if stages >= 3:
        ffn_call(False)

    join("sync", s_dve, s_act, s_pe, s_pool)
    for s in (d_st, d_ld, d_mw, d_mr, d_w, d_acc):
        R.wait("sync", s)

    with nc.Block() as block:
        R.replay(block)
    stck.close()
    return nc


def kernel(**inputs):
    shared = _prep_shared(inputs)
    x = np.asarray(inputs["L0_attn_x"])
    nc = _build()
    in_maps = []
    for b in range(8):
        m = dict(shared)
        m["xT"] = np.ascontiguousarray(x[b].T)
        in_maps.append(m)
    res = run_bass_kernel_spmd(nc, in_maps, core_ids=list(range(8)))
    out = np.stack([np.asarray(r["y"]).T for r in res.results], axis=0)
    return out.astype(np.float32)


# version 05 (final)
# speedup vs baseline: 1.0125x; 1.0125x over previous
import numpy as np
import ml_dtypes
from contextlib import ExitStack

import concourse.bass as bass
import concourse.mybir as mybir
from concourse.bass_utils import run_bass_kernel_spmd

F32 = mybir.dt.float32
BF16 = mybir.dt.bfloat16
AF = mybir.ActivationFunctionType
ALU = mybir.AluOpType
AX = mybir.AxisListType
BF = ml_dtypes.bfloat16

S = 2048
D = 1024
H = 16
HD = 64
FFD = 4096
NE = 8
NCH = 8
NTC = 4
MW = 2560
EPS = 1e-5

ENGS = ("tensor", "vector", "scalar", "gpsimd", "sync")


class Sem:
    def __init__(self, h, step=1):
        self.h = h
        self.step = step
        self.n = 0


class Rec:
    def __init__(self):
        self.q = {e: [] for e in ENGS}
        self.waited = {e: {} for e in ENGS}

    def op(self, eng, fn, inc=None):
        if inc is not None:
            inc.n += inc.step
            h, stp = inc.h, inc.step
            self.q[eng].append(lambda e, fn=fn, h=h, stp=stp: fn(e).then_inc(h, stp))
            return inc.n
        self.q[eng].append(lambda e, fn=fn: fn(e))
        return None

    def wait(self, eng, sem, val=None):
        v = sem.n if val is None else val
        if v <= 0:
            return
        if self.waited[eng].get(id(sem), -1) >= v:
            return
        self.waited[eng][id(sem)] = v
        self.q[eng].append(lambda e, h=sem.h, v=v: e.wait_ge(h, v))

    def replay(self, block):
        rq = self.q

        @block.tensor
        def _(e):
            for f in rq["tensor"]:
                f(e)

        @block.vector
        def _(e):
            for f in rq["vector"]:
                f(e)

        @block.scalar
        def _(e):
            for f in rq["scalar"]:
                f(e)

        @block.gpsimd
        def _(e):
            for f in rq["gpsimd"]:
                f(e)

        @block.sync
        def _(e):
            for f in rq["sync"]:
                f(e)


def _host_constants():
    c = {}
    pos = np.arange(S - 1, -1, -1, dtype=np.float64)
    inv_freq = 1.0 / 10000.0 ** (np.arange(0, D, 2, dtype=np.float64) / D)
    sinus = pos[:, None] * inv_freq[None, :]
    pe = np.concatenate([np.sin(sinus), np.cos(sinus)], axis=-1)
    c["c_pet"] = np.ascontiguousarray(pe.T).astype(np.float32)
    ang = np.arange(S, dtype=np.float64)[:, None] * (
        1.0 / 10000.0 ** (np.arange(0, HD, 2, dtype=np.float64) / HD))[None, :]
    cos64 = np.repeat(np.cos(ang), 2, axis=1).T
    sin64 = np.repeat(np.sin(ang), 2, axis=1).T
    sgn = np.where(np.arange(HD) % 2 == 0, -1.0, 1.0)[:, None]
    sinm64 = sin64 * sgn
    c["c_cos"] = np.vstack([cos64, cos64]).astype(BF)
    c["c_sinm"] = np.vstack([sinm64, sinm64]).astype(BF)
    c["c_id"] = np.eye(128, dtype=np.float32)
    pswap = np.zeros((128, 128), dtype=np.float32)
    for i in range(0, 128, 2):
        pswap[i + 1, i] = 1.0
        pswap[i, i + 1] = 1.0
    c["c_pswap"] = pswap.astype(BF)
    eye8 = np.zeros((NE, NE * 128), dtype=np.float32)
    for e in range(NE):
        eye8[e, e * 128:(e + 1) * 128] = 1.0
    c["c_eye8"] = eye8.astype(BF)
    return c


def _col8(v):
    return np.ascontiguousarray(v.reshape(8, 128).T)


def _prep_shared(w):
    m = {}
    m.update(_host_constants())
    m["WqT"] = np.ascontiguousarray(np.asarray(w["L0_attn_Wq"]).T)
    m["WkT"] = np.ascontiguousarray(np.asarray(w["L0_attn_Wk"]).T)
    m["WvT"] = np.ascontiguousarray(np.asarray(w["L0_attn_Wv"]).T)
    m["WoaT"] = np.ascontiguousarray(np.asarray(w["L0_attn_Wo"]).T)
    m["boc"] = _col8(np.asarray(w["L0_attn_bo"]))
    for k in ("Wqkv", "Wo", "Wr", "Wff1", "Wff2", "Wg", "We1", "We2", "Wout"):
        m[k] = np.ascontiguousarray(np.asarray(w["L0_ffn_" + k]))
    m["uc"] = _col8(np.asarray(w["L0_ffn_u_bias"]).reshape(-1))
    m["vc"] = _col8(np.asarray(w["L0_ffn_v_bias"]).reshape(-1))
    for i in (1, 2, 3):
        m[f"ln{i}g"] = _col8(np.asarray(w[f"L0_ffn_ln{i}_g"]))
        m[f"ln{i}b"] = _col8(np.asarray(w[f"L0_ffn_ln{i}_b"]))
    m["bff1c"] = np.ascontiguousarray(np.asarray(w["L0_ffn_bff1"]).reshape(32, 128).T)
    m["bff2c"] = _col8(np.asarray(w["L0_ffn_bff2"]))
    be1 = np.asarray(w["L0_ffn_be1"])
    m["be1c"] = np.ascontiguousarray(
        be1.reshape(NE, 32, 128).transpose(2, 0, 1).reshape(128, NE * 32))
    be2 = np.asarray(w["L0_ffn_be2"])
    m["be2c"] = np.ascontiguousarray(
        be2.reshape(NE, 8, 128).transpose(2, 0, 1).reshape(128, NE * 8))
    m["be2r"] = np.ascontiguousarray(be2)
    return m


def _build(taps=(), stages=3):
    taps = set(taps)
    nc = bass.Bass("TRN2", target_bir_lowering=False)

    def dram_in(name, shape, dtype=F32):
        return nc.dram_tensor(name, list(shape), dtype, kind="ExternalInput").ap()

    def dram_int(name, shape, dtype=F32):
        kind = "ExternalOutput" if name in taps else "Internal"
        return nc.dram_tensor(name, list(shape), dtype, kind=kind).ap()

    xin = dram_in("xT", (D, S))
    WqT = dram_in("WqT", (D, D)); WkT = dram_in("WkT", (D, D))
    WvT = dram_in("WvT", (D, D)); WoaT = dram_in("WoaT", (D, D))
    Wqkv = dram_in("Wqkv", (D, 3 * D)); Wo = dram_in("Wo", (D, D))
    Wr = dram_in("Wr", (D, D)); Wff1 = dram_in("Wff1", (D, FFD))
    Wff2 = dram_in("Wff2", (FFD, D)); Wg = dram_in("Wg", (D, NE))
    We1 = dram_in("We1", (NE, D, FFD)); We2 = dram_in("We2", (NE, FFD, D))
    Wout = dram_in("Wout", (D, D))
    boc = dram_in("boc", (128, 8)); uc = dram_in("uc", (128, 8))
    vc = dram_in("vc", (128, 8))
    lng = {i: dram_in(f"ln{i}g", (128, 8)) for i in (1, 2, 3)}
    lnb = {i: dram_in(f"ln{i}b", (128, 8)) for i in (1, 2, 3)}
    bff1c = dram_in("bff1c", (128, 32))
    bff2c = dram_in("bff2c", (128, 8))
    be1c = dram_in("be1c", (128, NE * 32))
    be2c = dram_in("be2c", (128, NE * 8))
    be2r = dram_in("be2r", (NE, D))
    c_pet = dram_in("c_pet", (D, S))
    c_cos = dram_in("c_cos", (128, S), BF16)
    c_sinm = dram_in("c_sinm", (128, S), BF16)
    c_id = dram_in("c_id", (128, 128))
    c_pswap = dram_in("c_pswap", (128, 128), BF16)
    c_eye8 = dram_in("c_eye8", (NE, NE * 128), BF16)

    yout = nc.dram_tensor("y", [D, S], F32, kind="ExternalOutput").ap()

    r_d = dram_int("r_d", (D, S), BF16)
    qu_d = dram_int("qu_d", (D, S), BF16)
    qv_d = dram_int("qv_d", (D, S), BF16)
    kk_d = dram_int("kk_d", (D, S), BF16)
    v_d = dram_int("v_d", (S, D), BF16)
    M_d = nc.dram_tensor("M_d", [2, 512, MW], F32, kind="Internal").ap()

    A = nc.alloc_sbuf_tensor
    a1 = [A(f"a1_{c}", [128, S], BF16) for c in range(NCH)]
    mid = [A(f"mid{i}", [128, 512], BF16) for i in range(32)]
    bdq = [A(f"bdq_{i}", [128, 512], F32) for i in range(8)]
    pacc = [A(f"pacc{i}", [128, 1024], BF16) for i in range(NE)]
    hid2 = [A(f"hid2_{i}", [128, 1024], BF16) for i in range(8)]
    hctx = [A(f"hctx{c}", [128, 512], BF16) for c in range(NCH)]
    wring = [A(f"wr{i}", [128, 1024], BF16) for i in range(6)]
    kr_b = [A(f"krb{i}", [128, S], BF16) for i in range(3)]
    xstage = [A(f"xst{i}", [128, 512], F32) for i in range(2)]
    cos_sb = A("cos_sb", [128, S], BF16)
    sinm_sb = A("sinm_sb", [128, S], BF16)
    voh_b = [A(f"voh{i}", [128, H * 65], BF16) for i in range(2)]
    mstage = [A(f"mst{i}", [128, 512], F32) for i in range(2)]
    lnrow = A("lnrow", [128, 2048], F32)
    wT_bf = A("wT_bf", [NE, S], BF16)
    quv_b = [A(f"quvb{i}", [128, 512], BF16) for i in range(3)]
    accst = [A(f"acc{i}", [128, 512], F32) for i in range(2)]
    ptile = [A(f"pt{i}", [128, 512], BF16) for i in range(3)]
    qstage = [A(f"qst{i}", [128, 512], BF16) for i in range(2)]
    rtmp = [A(f"rtmp{i}", [128, 512], BF16) for i in range(2)]
    ttmp = A("ttmp", [128, 512], F32)
    sqst = [A(f"sqst{i}", [128, 512], BF16) for i in range(2)]
    eye8_sb = A("eye8_sb", [NE, NE * 128], BF16)
    be2r_sb = A("be2r_sb", [NE, D], BF16)
    wbc_sb = A("wbc_sb", [128, 1024], BF16)
    recip_row = A("recip_row", [1, 512], F32)
    bcsb = A("bcsb", [64, 512], BF16)
    id_sb = A("id_sb", [128, 128], F32)
    ones_f = A("ones_f", [1, 128], F32)
    ones_bf = A("ones_bf", [1, 128], BF16)
    onesc_bf = A("onesc_bf", [128, 1], BF16)
    pswap_sb = A("pswap_sb", [128, 128], BF16)
    uvb_sb = A("uvb_sb", [128, 32], F32)
    lnp_sb = A("lnp_sb", [128, 48], F32)
    bff1_sb = A("bff1_sb", [128, 32], F32)
    be1_sb = A("be1_sb", [128, NE * 32], F32)
    be2_sb = A("be2_sb", [128, NE * 8], F32)
    wg_sb = A("wg_sb", [128, 64], BF16)
    rt = A("rt", [128, 64], F32)
    cvals = A("cvals", [128, 8], F32)

    pb = [nc.alloc_psum_tensor(f"pb{i}", [128, 512], F32) for i in range(8)]

    stck = ExitStack()
    sems = {}
    names = [("s_pe", 1), ("s_act", 1), ("s_dve", 1), ("s_pool", 1),
             ("d_w", 16), ("d_acc", 16), ("d_ld", 16), ("d_st", 16),
             ("d_mw", 16), ("d_mr", 16), ("d_mid", 16)]
    names += [(f"w{i}", 16) for i in range(6)]
    names += [(f"hl{i}", 16) for i in range(3)]
    names += [(f"vo{i}", 16) for i in range(2)]
    names += [(f"xs{i}", 16) for i in range(2)]
    names += [(f"mr{i}", 16) for i in range(2)]
    names += [(f"qs{i}", 16) for i in range(2)]
    names += [("rts", 16)]
    names += [(f"ac{i}", 16) for i in range(2)]
    for name, step in names:
        sems[name] = Sem(stck.enter_context(nc.semaphore(name)), step)
    s_pe, s_act, s_dve, s_pool = (sems[k] for k in ("s_pe", "s_act", "s_dve", "s_pool"))
    d_w, d_acc, d_ld, d_st, d_mw, d_mr = (
        sems[k] for k in ("d_w", "d_acc", "d_ld", "d_st", "d_mw", "d_mr"))
    d_mid = sems["d_mid"]
    w_sem = [sems[f"w{i}"] for i in range(6)]
    hl_sem = [sems[f"hl{i}"] for i in range(3)]
    vo_sem = [sems[f"vo{i}"] for i in range(2)]
    xs_sem = [sems[f"xs{i}"] for i in range(2)]
    mr_sem = [sems[f"mr{i}"] for i in range(2)]
    qs_sem = [sems[f"qs{i}"] for i in range(2)]
    rts = sems["rts"]
    ac_sem = [sems[f"ac{i}"] for i in range(2)]

    R = Rec()
    bank_free = [None] * 8
    bank_prod = [None] * 8

    def pe_begin_bank(i):
        if bank_free[i] is not None:
            R.wait("tensor", bank_free[i][0], bank_free[i][1])

    def pe_end_bank(i):
        bank_prod[i] = (s_pe, s_pe.n)

    def consume_bank(i, eng):
        R.wait(eng, bank_prod[i][0], bank_prod[i][1])

    def consumed_bank(i, sem):
        bank_free[i] = (sem, sem.n)

    def join(eng, *ss):
        for s in ss:
            R.wait(eng, s)

    ring_state = [{"loaded": 0, "lastuse": 0} for _ in wring]
    ring_ptr = [0]

    def wload_cols(dram, coloff, ncols, nrows, rowstride, rowoff=0):
        slot = ring_ptr[0]
        ring_ptr[0] = (slot + 1) % len(wring)
        stt = ring_state[slot]
        if stt["lastuse"]:
            R.wait("gpsimd", s_pe, stt["lastuse"])
        nfc = nrows // 128
        src = bass.AP(dram.tensor, rowoff * rowstride + coloff,
                      [[rowstride, 128], [128 * rowstride, nfc], [1, ncols]])
        R.op("gpsimd", lambda e, d=wring[slot][:, 0:nfc * ncols], s=src: e.dma_start(d, s), inc=w_sem[slot])
        stt["loaded"] = w_sem[slot].n
        return slot

    def ring_used(slot):
        ring_state[slot]["lastuse"] = s_pe.n

    for c in range(NCH):
        for tc in range(NTC):
            p = (c * NTC + tc) % 2
            join("sync", d_st)
            R.op("sync", lambda e, c=c, tc=tc, p=p: e.dma_start(
                accst[p][:, :], xin[c * 128:(c + 1) * 128, tc * 512:(tc + 1) * 512]), inc=d_ld)
            join("sync", d_ld)
            R.op("sync", lambda e, c=c, tc=tc, p=p: e.dma_start(
                yout[c * 128:(c + 1) * 128, tc * 512:(tc + 1) * 512], accst[p][:, :]), inc=d_st)
    for dst, src in ((cos_sb, c_cos), (sinm_sb, c_sinm), (id_sb, c_id),
                     (pswap_sb, c_pswap), (bff1_sb, bff1c), (be1_sb, be1c),
                     (be2_sb, be2c), (eye8_sb, c_eye8)):
        R.op("sync", lambda e, d=dst, s=src: e.dma_start(d[:, :], s[:, :]), inc=d_ld)
    R.op("gpsimd", lambda e: e.dma_start(be2r_sb[:, :], be2r[:, :]), inc=d_mid)
    wg_src = bass.AP(Wg.tensor, 0, [[NE, 128], [128 * NE, 8], [1, NE]])
    R.op("gpsimd", lambda e: e.dma_start(wg_sb[:, :], wg_src), inc=d_mid)
    R.op("sync", lambda e: e.dma_start(uvb_sb[:, 0:8], uc[:, :]), inc=d_ld)
    R.op("sync", lambda e: e.dma_start(uvb_sb[:, 8:16], vc[:, :]), inc=d_ld)
    R.op("sync", lambda e: e.dma_start(uvb_sb[:, 16:24], boc[:, :]), inc=d_ld)
    R.op("sync", lambda e: e.dma_start(uvb_sb[:, 24:32], bff2c[:, :]), inc=d_ld)
    for i, k in ((0, 1), (2, 2), (4, 3)):
        R.op("sync", lambda e, i=i, k=k: e.dma_start(lnp_sb[:, i * 8:(i + 1) * 8], lng[k][:, :]), inc=d_ld)
        R.op("sync", lambda e, i=i, k=k: e.dma_start(lnp_sb[:, (i + 1) * 8:(i + 2) * 8], lnb[k][:, :]), inc=d_ld)
    join("gpsimd", d_ld)
    for i, v in ((0, EPS), (1, -0.5), (2, 0.125), (3, 1.0 / 64), (4, -1.0),
                 (5, 1.0), (6, 0.0)):
        R.op("gpsimd", lambda e, i=i, v=v: e.memset(cvals[:, i:i + 1], v))
    R.op("gpsimd", lambda e: e.memset(ones_f[:, :], 1.0))
    R.op("gpsimd", lambda e: e.memset(ones_bf[:, :], 1.0))
    R.op("gpsimd", lambda e: e.memset(onesc_bf[:, :], 1.0))
    R.op("gpsimd", lambda e: e.iota(rt[:, 48:56], [[1, NE]], channel_multiplier=0,
                                    allow_small_or_imprecise_dtypes=True), inc=s_pool)
    R.wait("gpsimd", s_pool, s_pool.n)
    R.op("gpsimd", lambda e: e.tensor_scalar(rt[:, 56:64], rt[:, 48:56], -1.0, -16384.0, ALU.mult, ALU.add))
    R.op("gpsimd", lambda e: e.tensor_scalar(uvb_sb[:, 0:16], uvb_sb[:, 0:16], 0.125, None, ALU.mult))
    R.op("gpsimd", lambda e: e.tensor_scalar(lnp_sb[:, 32:48], lnp_sb[:, 32:48], 0.5, None, ALU.mult))
    for b in range(2):
        R.op("gpsimd", lambda e, b=b: e.memset(voh_b[b][:, :], 1.0))
    R.op("gpsimd", lambda e: e.memset(mstage[0][:, :], 0.0), inc=s_pool)
    join("sync", s_pool)
    for reg in range(2):
        for rb_ in range(4):
            dst = bass.AP(M_d.tensor, reg * 512 * MW + rb_ * 128 * MW + 2048,
                          [[MW, 128], [1, 512]])
            R.op("sync", lambda e, d=dst: e.dma_start(d, mstage[0][:, :]), inc=d_st)

    for fc in range(NCH):
        for jc in range(4):
            R.op("gpsimd", lambda e, fc=fc, jc=jc: e.dma_start(
                mid[fc * 4 + jc][:, :],
                c_pet[fc * 128:(fc + 1) * 128, jc * 512:(jc + 1) * 512]), inc=d_mid)
    pe_loads = d_mid.n
    for oc in range(NCH):
        slot = wload_cols(Wr, oc * 128, 128, D, D)
        for jc in range(4):
            bank = (oc * 4 + jc) % 2
            pe_begin_bank(bank)
            R.wait("tensor", w_sem[slot], ring_state[slot]["loaded"])
            R.wait("tensor", d_mid, pe_loads)
            for fc in range(NCH):
                last = fc == NCH - 1
                R.op("tensor", lambda e, b=bank, s=slot, fc=fc, jc=jc, last=last: e.matmul(
                    pb[b][:, :], wring[s][:, fc * 128:(fc + 1) * 128],
                    mid[fc * 4 + jc][:, :], start=(fc == 0), stop=last),
                    inc=(s_pe if last else None))
            pe_end_bank(bank)
            ring_used(slot)
            consume_bank(bank, "scalar")
            q = (oc * 4 + jc) % 2
            R.wait("scalar", qs_sem[q], qs_sem[q].n)
            R.op("scalar", lambda e, b=bank, q=q: e.copy(qstage[q][:, :], pb[b][:, :]), inc=s_act)
            consumed_bank(bank, s_act)
            join("sync", s_act)
            R.op("sync", lambda e, oc=oc, jc=jc, q=q: e.dma_start(
                bass.AP(r_d.tensor, oc * 128 * S + jc * 512, [[S, 128], [1, 512]]),
                qstage[q][:, :]), inc=qs_sem[q])

    def lr(cb):
        return lnrow[0:1, cb:cb + 512]

    LR_M, LR_T, LR_B, LR_RSTD = 0, 512, 1024, 1536

    def vsync():
        R.wait("vector", s_dve, s_dve.n)

    def ssync():
        R.wait("scalar", s_act, s_act.n)

    def ln_rows():
        consume_bank(6, "vector")
        consume_bank(7, "vector")
        R.op("vector", lambda e: e.tensor_scalar(lr(LR_M), pb[6][0:1, :], 1.0 / D, None, ALU.mult), inc=s_dve)
        vsync()
        R.op("vector", lambda e: e.tensor_mul(lr(LR_T), lr(LR_M), lr(LR_M)), inc=s_dve)
        vsync()
        R.op("vector", lambda e: e.scalar_tensor_tensor(
            lr(LR_T), pb[7][0:1, :], 1.0 / D, lr(LR_T), ALU.mult, ALU.subtract), inc=s_dve)
        consumed_bank(6, s_dve)
        consumed_bank(7, s_dve)
        join("scalar", s_dve)
        R.op("scalar", lambda e: e.activation(lr(LR_B), lr(LR_T), AF.Ln, bias=cvals[0:1, 0:1], scale=cvals[0:1, 5:6]), inc=s_act)
        ssync()
        R.op("scalar", lambda e: e.activation(lr(LR_RSTD), lr(LR_B), AF.Exp, bias=cvals[0:1, 6:7], scale=cvals[0:1, 1:2]), inc=s_act)
        join("vector", s_act)
        R.op("vector", lambda e: e.tensor_mul(lr(LR_B), lr(LR_M), lr(LR_RSTD)), inc=s_dve)
        return s_dve.n

    def ln_bcast(rv):
        R.wait("tensor", s_dve, rv)
        pe_begin_bank(6)
        pe_begin_bank(7)
        R.op("tensor", lambda e: e.matmul(pb[6][:, :], ones_f[0:1, :], lr(LR_RSTD), start=True, stop=True))
        R.op("tensor", lambda e: e.matmul(pb[7][:, :], ones_f[0:1, :], lr(LR_B), start=True, stop=True), inc=s_pe)
        pe_end_bank(6)
        pe_end_bank(7)

    sq_mark = {0: 0, 1: 0}

    def ln_stats_chunk(src_bf, c, sqslot):
        R.wait("scalar", s_pe, sq_mark[c % 2])
        R.op("scalar", lambda e, s=src_bf, q=sqslot: e.activation(q, s, AF.Square), inc=s_act)
        if c == 0:
            pe_begin_bank(6)
            pe_begin_bank(7)
        last = c == NCH - 1
        join("tensor", s_act)
        R.op("tensor", lambda e, s=src_bf, c=c, last=last: e.matmul(
            pb[6][0:1, :], onesc_bf[:, 0:1], s, start=(c == 0), stop=last,
            skip_group_check=True))
        R.op("tensor", lambda e, q=sqslot, c=c, last=last: e.matmul(
            pb[7][0:1, :], onesc_bf[:, 0:1], q, start=(c == 0), stop=last,
            skip_group_check=True), inc=s_pe)
        sq_mark[c % 2] = s_pe.n
        if last:
            pe_end_bank(6)
            pe_end_bank(7)

    def emit_ln0():
        for tc in range(NTC):
            cols = slice(tc * 512, (tc + 1) * 512)
            join("gpsimd", ac_sem[0], ac_sem[1], d_st, s_pe, s_act)
            for c in range(NCH):
                R.op("gpsimd", lambda e, c=c, cols=cols: e.dma_start(
                    mid[c][:, :], yout[c * 128:(c + 1) * 128, cols]), inc=d_mid)
            join("scalar", d_mid)
            for c in range(NCH):
                ln_stats_chunk(mid[c][:, :], c, sqst[c % 2][:, :])
            rv = ln_rows()
            ln_bcast(rv)
            for c in range(NCH):
                p = c % 2
                join("sync", s_dve, ac_sem[0], ac_sem[1], d_st)
                R.op("sync", lambda e, c=c, cols=cols, p=p: e.dma_start(
                    xstage[p][:, :], yout[c * 128:(c + 1) * 128, cols]), inc=xs_sem[p])
                consume_bank(6, "vector")
                consume_bank(7, "vector")
                R.wait("vector", xs_sem[p], xs_sem[p].n)
                R.op("vector", lambda e, p=p: e.tensor_mul(ttmp[:, :], xstage[p][:, :], pb[6][:, :]))
                R.op("vector", lambda e, c=c, cols=cols: e.tensor_sub(
                    a1[c][:, cols], ttmp[:, :], pb[7][:, :]), inc=s_dve)
            consumed_bank(6, s_dve)
            consumed_bank(7, s_dve)

    def proj_group(bank, slot, src_chunks, cols):
        pe_begin_bank(bank)
        R.wait("tensor", w_sem[slot], ring_state[slot]["loaded"])
        join("tensor", s_dve, s_act)
        for fc in range(NCH):
            last = fc == NCH - 1
            R.op("tensor", lambda e, b=bank, s=slot, fc=fc, cols=cols, last=last,
                 src_chunks=src_chunks: e.matmul(
                pb[b][:, :], wring[s][:, fc * 128:(fc + 1) * 128],
                src_chunks[fc][:, cols], start=(fc == 0), stop=last),
                inc=(s_pe if last else None))
        pe_end_bank(bank)
        ring_used(slot)

    def emit_qkv_txl():
        for oc in range(16):
            slot = wload_cols(Wqkv, oc * 128, 128, D, 3 * D)
            for tc in range(NTC):
                bank = (oc * NTC + tc) % 2
                proj_group(bank, slot, a1, slice(tc * 512, (tc + 1) * 512))
                qs = (oc * NTC + tc) % 2
                if oc < 8:
                    for dram, boff in ((qu_d, 0), (qv_d, 8)):
                        consume_bank(bank, "scalar")
                        join("scalar", d_st)
                        R.op("scalar", lambda e, b=bank, qs=qs, oc=oc, boff=boff: e.activation(
                            qstage[qs][:, :], pb[b][:, :], AF.Identity,
                            bias=uvb_sb[:, boff + oc:boff + oc + 1],
                            scale=cvals[:, 2:3]), inc=s_act)
                        consumed_bank(bank, s_act)
                        join("sync", s_act)
                        R.op("sync", lambda e, d=dram, oc=oc, tc=tc, qs=qs: e.dma_start(
                            bass.AP(d.tensor, oc * 128 * S + tc * 512, [[S, 128], [1, 512]]),
                            qstage[qs][:, :]), inc=d_st)
                else:
                    consume_bank(bank, "scalar")
                    join("scalar", d_st)
                    R.op("scalar", lambda e, b=bank, qs=qs: e.copy(qstage[qs][:, :], pb[b][:, :]), inc=s_act)
                    consumed_bank(bank, s_act)
                    join("sync", s_act)
                    R.op("sync", lambda e, oc=oc, tc=tc, qs=qs: e.dma_start(
                        bass.AP(kk_d.tensor, (oc - 8) * 128 * S + tc * 512, [[S, 128], [1, 512]]),
                        qstage[qs][:, :]), inc=d_st)

    def emit_qkv_v(WsrcT=None):
        join("gpsimd", s_pe, s_act)
        for vc in range(2):
            base = 16 if vc == 0 else 24
            for fc in range(NCH):
                if WsrcT is None:
                    src = Wqkv[fc * 128:(fc + 1) * 128, 2 * D + vc * 512:2 * D + (vc + 1) * 512]
                else:
                    src = WsrcT[fc * 128:(fc + 1) * 128, vc * 512:(vc + 1) * 512]
                R.op("gpsimd", lambda e, fc=fc, base=base, src=src: e.dma_start(
                    mid[base + fc][:, :], src), inc=d_mid)
        vw = d_mid.n
        for tt in range(16):
            for vc in range(2):
                base = 16 if vc == 0 else 24
                bank = (tt * 2 + vc) % 2
                pe_begin_bank(bank)
                R.wait("tensor", d_mid, vw)
                join("tensor", s_dve)
                for fc in range(NCH):
                    last = fc == NCH - 1
                    R.op("tensor", lambda e, b=bank, fc=fc, tt=tt, base=base, last=last: e.matmul(
                        pb[b][:, :], a1[fc][:, tt * 128:(tt + 1) * 128],
                        mid[base + fc][:, :], start=(fc == 0), stop=last),
                        inc=(s_pe if last else None))
                pe_end_bank(bank)
                qs = (tt * 2 + vc) % 2
                consume_bank(bank, "scalar")
                R.wait("scalar", qs_sem[qs], qs_sem[qs].n)
                R.op("scalar", lambda e, b=bank, qs=qs: e.copy(qstage[qs][:, :], pb[b][:, :]), inc=s_act)
                consumed_bank(bank, s_act)
                join("sync", s_act)
                R.op("sync", lambda e, tt=tt, vc=vc, qs=qs: e.dma_start(
                    bass.AP(v_d.tensor, tt * 128 * D + vc * 512, [[D, 128], [1, 512]]),
                    qstage[qs][:, :]), inc=qs_sem[qs])

    def emit_qkv_rope():
        for WT, dram, scl in ((WqT, qu_d, "q"), (WkT, kk_d, "k")):
            for oc in range(8):
                slot = wload_cols(WT, oc * 128, 128, D, D)
                for tc in range(NTC):
                    bank = (oc * NTC + tc) % 2
                    proj_group(bank, slot, a1, slice(tc * 512, (tc + 1) * 512))
                    qs = (oc * NTC + tc) % 2
                    consume_bank(bank, "scalar")
                    join("scalar", s_pe, s_dve)
                    R.wait("scalar", qs_sem[qs], qs_sem[qs].n)
                    scl_ap = cvals[:, 3:4] if scl == "q" else cvals[:, 5:6]
                    R.op("scalar", lambda e, b=bank, qs=qs, sa=scl_ap: e.activation(
                        qstage[qs][:, :], pb[b][:, :], AF.Identity, bias=cvals[:, 6:7], scale=sa), inc=s_act)
                    pe_begin_bank(4)
                    join("tensor", s_act)
                    R.op("tensor", lambda e, qs=qs: e.matmul(
                        pb[4][:, :], pswap_sb[:, :], qstage[qs][:, :], start=True, stop=True), inc=s_pe)
                    pe_end_bank(4)
                    consumed_bank(bank, s_pe)
                    consume_bank(4, "vector")
                    tcs = slice(tc * 512, (tc + 1) * 512)
                    R.wait("vector", rts, rts.n)
                    R.op("vector", lambda e, tcs=tcs: e.tensor_mul(rtmp[0][:, :], pb[4][:, :], sinm_sb[:, tcs]))
                    R.op("vector", lambda e, qs=qs, tcs=tcs: e.tensor_mul(rtmp[1][:, :], qstage[qs][:, :], cos_sb[:, tcs]))
                    R.op("vector", lambda e: e.tensor_add(rtmp[0][:, :], rtmp[0][:, :], rtmp[1][:, :]), inc=s_dve)
                    consumed_bank(4, s_dve)
                    join("sync", s_dve)
                    R.op("sync", lambda e, d=dram, oc=oc, tc=tc: e.dma_start(
                        bass.AP(d.tensor, oc * 128 * S + tc * 512, [[S, 128], [1, 512]]),
                        rtmp[0][:, :]), inc=rts)

    def emit_attention(txl, epilogue):
        head_guard = {0: 0, 1: 0, 2: 0}
        voh_guard = {0: 0, 1: 0}
        bdq_free = {0: s_pe.n, 1: s_pe.n}
        reg_rb_mark = {0: (0, 0), 1: (0, 0)}
        rb_mark = {}
        hctx_guard = [s_pe.n]
        ctx_guard = [s_pe.n]
        join("sync", d_st, qs_sem[0], qs_sem[1], rts)

        def nk_of(qc):
            return 4 * (qc + 1)

        def head_load(it):
            qc, h = it // 16, it % 16
            kb = it % 3
            R.wait("sync", s_pe, head_guard[kb])
            sm = hl_sem[kb]
            R.op("sync", lambda e, h=h, kb=kb: e.dma_start(
                kr_b[kb][0:64, :], kk_d[h * 64:(h + 1) * 64, :]), inc=sm)
            if txl:
                R.op("sync", lambda e, h=h, kb=kb: e.dma_start(
                    kr_b[kb][64:128, :], r_d[h * 64:(h + 1) * 64, :]), inc=sm)
            R.op("sync", lambda e, h=h, kb=kb, qc=qc: e.dma_start(
                quv_b[kb][0:64, :], qu_d[h * 64:(h + 1) * 64, qc * 512:(qc + 1) * 512]), inc=sm)
            if txl:
                R.op("sync", lambda e, h=h, kb=kb, qc=qc: e.dma_start(
                    quv_b[kb][64:128, :], qv_d[h * 64:(h + 1) * 64, qc * 512:(qc + 1) * 512]), inc=sm)

        def voh_load(it):
            qc, h = it // 16, it % 16
            vb = it % 2
            R.wait("sync", s_pe, voh_guard[vb])
            for g4 in range(4):
                src = bass.AP(v_d.tensor, (g4 * 4) * 128 * D + h * 64,
                              [[D, 128], [128 * D, 4], [1, 64]])
                dst = bass.AP(voh_b[vb].ap().tensor, (g4 * 4) * 65,
                              [[H * 65, 128], [65, 4], [1, 64]])
                R.op("sync", lambda e, d=dst, s=src: e.dma_start(d, s), inc=vo_sem[vb])

        def emit_M(it):
            qc, h = it // 16, it % 16
            reg = it % 2
            kb = it % 3
            jlo = 1536 - 512 * qc
            njc = qc + 1
            R.wait("tensor", hl_sem[kb], head_ld_mark[it])
            R.wait("sync", mr_sem[0], reg_rb_mark[reg][0])
            R.wait("sync", mr_sem[1], reg_rb_mark[reg][1])
            for qs in range(4):
                for jc in range(njc):
                    bank = 3 + (qs * njc + jc) % 2
                    pe_begin_bank(bank)
                    R.op("tensor", lambda e, b=bank, kb=kb, qs=qs, jl=jlo + jc * 512: e.matmul(
                        pb[b][:, :], quv_b[kb][64:128, qs * 128:(qs + 1) * 128],
                        kr_b[kb][64:128, jl:jl + 512], start=True, stop=True), inc=s_pe)
                    pe_end_bank(bank)
                    ms = (qs * njc + jc) % 2
                    consume_bank(bank, "scalar")
                    join("scalar", d_mw)
                    R.op("scalar", lambda e, b=bank, ms=ms: e.copy(mstage[ms][:, :], pb[b][:, :]), inc=s_act)
                    consumed_bank(bank, s_act)
                    join("sync", s_act)
                    dst = bass.AP(M_d.tensor, reg * 512 * MW + qs * 128 * MW + jlo + jc * 512,
                                  [[MW, 128], [1, 512]])
                    R.op("sync", lambda e, d=dst, ms=ms: e.dma_start(d, mstage[ms][:, :]), inc=d_mw)

        def issue_rb(it, g):
            qc = it // 16
            reg = it % 2
            p = g % 2
            R.wait("sync", s_pe, bdq_free[p])
            for qs in range(4):
                off = reg * 512 * MW + qs * 128 * MW + (2047 - qc * 512 - qs * 128 + 512 * g)
                src = bass.AP(M_d.tensor, off, [[MW - 1, 128], [1, 512]])
                R.op("sync", lambda e, s=src, g=g, qs=qs: e.dma_start(
                    bdq[p * 4 + qs][:, :], s), inc=mr_sem[p])
            rb_mark[g] = mr_sem[p].n

        def emit_head(it):
            qc, h = it // 16, it % 16
            nk = nk_of(qc)
            kb = it % 3
            vb = it % 2
            R.wait("tensor", hl_sem[kb], head_ld_mark[it])
            R.wait("tensor", vo_sem[vb], voh_ld_mark[it])
            for kt in range(nk):
                g = kt // 4
                stb = kt % 2
                if txl and kt % 4 == 0 and g + 1 <= (nk - 1) // 4:
                    issue_rb(it, g + 1)
                pe_begin_bank(stb)
                R.op("tensor", lambda e, b=stb, kb=kb, kt=kt, st=(not txl): e.matmul(
                    pb[b][:, :], kr_b[kb][0:64, kt * 128:(kt + 1) * 128],
                    quv_b[kb][0:64, :], start=True, stop=(not txl),
                    skip_group_check=True), inc=(None if txl else s_pe))
                if txl:
                    R.wait("tensor", mr_sem[g % 2], rb_mark[g])
                    kl = (kt % 4) * 128
                    for qs in range(4):
                        R.op("tensor", lambda e, b=stb, g=g, qs=qs, kl=kl: e.matmul(
                            pb[b][:, qs * 128:(qs + 1) * 128],
                            bdq[(g % 2) * 4 + qs][:, kl:kl + 128],
                            id_sb[:, :], start=False, stop=(qs == 3),
                            is_transpose=True, skip_group_check=True),
                            inc=(s_pe if qs == 3 else None))
                pe_end_bank(stb)
                if txl:
                    bdq_free[g % 2] = s_pe.n
                consume_bank(stb, "scalar")
                join("scalar", s_pool)
                R.op("scalar", lambda e, b=stb, p=kt % 2: e.activation(
                    ptile[p][:, :], pb[b][:, :], AF.Exp), inc=s_act)
                consumed_bank(stb, s_act)
                use_p = kt % 2
                if kt >= nk - 4:
                    delta = kt * 128 - qc * 512
                    join("gpsimd", s_act)
                    R.wait("gpsimd", s_pe, ctx_guard[0])
                    R.op("gpsimd", lambda e, p=kt % 2, delta=delta: e.affine_select(
                        ptile[2][:, :], ptile[p][:, :], [[1, 512]], ALU.is_ge,
                        0.0, base=-delta, channel_multiplier=-1), inc=s_pool)
                    use_p = 2
                pe_begin_bank(2)
                if use_p == 2:
                    join("tensor", s_pool)
                else:
                    join("tensor", s_act)
                R.op("tensor", lambda e, vb=vb, h=h, p=use_p, kt=kt, nk=nk: e.matmul(
                    pb[2][0:65, :], voh_b[vb][:, kt * 65:(kt + 1) * 65],
                    ptile[p][:, :], start=(kt == 0), stop=(kt == nk - 1),
                    skip_group_check=True), inc=s_pe)
                if use_p == 2:
                    ctx_guard[0] = s_pe.n
                pe_end_bank(2)
            consume_bank(2, "vector")
            R.op("vector", lambda e: e.reciprocal(recip_row[:, :], pb[2][64:65, :]), inc=s_dve)
            pe_begin_bank(5)
            R.wait("tensor", s_dve, s_dve.n)
            R.op("tensor", lambda e: e.matmul(pb[5][0:64, :], ones_f[0:1, 0:64],
                                              recip_row[0:1, :], start=True, stop=True), inc=s_pe)
            pe_end_bank(5)
            consume_bank(5, "scalar")
            join("scalar", s_dve)
            R.op("scalar", lambda e: e.copy(bcsb[:, :], pb[5][0:64, :]), inc=s_act)
            consumed_bank(5, s_act)
            ch, half = h // 2, (h % 2) * 64
            join("vector", s_act)
            R.wait("vector", s_pe, hctx_guard[0])
            R.op("vector", lambda e, ch=ch, half=half: e.tensor_mul(
                hctx[ch][half:half + 64, :], pb[2][0:64, :], bcsb[:, :]), inc=s_dve)
            consumed_bank(2, s_dve)

        head_ld_mark = {}
        voh_ld_mark = {}

        head_load(0); head_ld_mark[0] = hl_sem[0].n
        voh_load(0); voh_ld_mark[0] = vo_sem[0].n
        if txl:
            emit_M(0)
            join("sync", d_mw)
            issue_rb(0, 0)
        head_load(1); head_ld_mark[1] = hl_sem[1].n
        voh_load(1); voh_ld_mark[1] = vo_sem[1].n

        for it in range(64):
            qc, h = it // 16, it % 16
            if txl and it + 1 < 64:
                emit_M(it + 1)
            if it + 2 < 64:
                head_load(it + 2); head_ld_mark[it + 2] = hl_sem[(it + 2) % 3].n
            emit_head(it)
            head_guard[it % 3] = s_pe.n
            voh_guard[it % 2] = s_pe.n
            if it + 2 < 64:
                voh_load(it + 2); voh_ld_mark[it + 2] = vo_sem[it % 2].n
            if txl:
                reg_rb_mark[it % 2] = (mr_sem[0].n, mr_sem[1].n)
            if txl and it + 1 < 64:
                join("sync", d_mw)
                issue_rb(it + 1, 0)
            if h == 15:
                epilogue(qc)
                hctx_guard[0] = s_pe.n

    def ffn_epilogue(qc):
        cols = slice(qc * 512, (qc + 1) * 512)
        for oc in range(NCH):
            slot = wload_cols(Wo, oc * 128, 128, D, D)
            bank = oc % 2
            pe_begin_bank(bank)
            R.wait("tensor", w_sem[slot], ring_state[slot]["loaded"])
            join("tensor", s_dve)
            for dc in range(NCH):
                last = dc == NCH - 1
                R.op("tensor", lambda e, b=bank, s=slot, dc=dc, last=last: e.matmul(
                    pb[b][:, :], wring[s][:, dc * 128:(dc + 1) * 128],
                    hctx[dc][:, :], start=(dc == 0), stop=last),
                    inc=(s_pe if last else None))
            pe_end_bank(bank)
            ring_used(slot)
            consume_bank(bank, "vector")
            R.op("vector", lambda e, b=bank, oc=oc, cols=cols: e.tensor_add(
                a1[oc][:, cols], pb[b][:, :], a1[oc][:, cols]), inc=s_dve)
            consumed_bank(bank, s_dve)
        emit_ln_inplace(cols, 0, 8)

    def emit_ln_inplace(cols, gcol, bcol):
        join("scalar", s_dve)
        for c in range(NCH):
            ln_stats_chunk(a1[c][:, cols], c, sqst[c % 2][:, :])
        rv = ln_rows()
        ln_bcast(rv)
        for c in range(NCH):
            consume_bank(6, "vector")
            consume_bank(7, "vector")
            join("vector", s_act)
            R.op("vector", lambda e, c=c, cols=cols: e.tensor_mul(ttmp[:, :], a1[c][:, cols], pb[6][:, :]))
            R.op("vector", lambda e, c=c: e.tensor_sub(xstage[0][:, :], ttmp[:, :], pb[7][:, :]), inc=s_dve)
            join("scalar", s_dve)
            R.op("scalar", lambda e, c=c, cols=cols, g=gcol, b=bcol: e.activation(
                a1[c][:, cols], xstage[0][:, :], AF.Identity,
                bias=lnp_sb[:, b + c:b + c + 1], scale=lnp_sb[:, g + c:g + c + 1]), inc=s_act)
        consumed_bank(6, s_dve)
        consumed_bank(7, s_dve)

    def attn_epilogue(qc):
        cols = slice(qc * 512, (qc + 1) * 512)
        for oc in range(NCH):
            slot = wload_cols(WoaT, oc * 128, 128, D, D)
            bank = oc % 2
            pe_begin_bank(bank)
            R.wait("tensor", w_sem[slot], ring_state[slot]["loaded"])
            join("tensor", s_dve)
            for dc in range(NCH):
                last = dc == NCH - 1
                R.op("tensor", lambda e, b=bank, s=slot, dc=dc, last=last: e.matmul(
                    pb[b][:, :], wring[s][:, dc * 128:(dc + 1) * 128],
                    hctx[dc][:, :], start=(dc == 0), stop=last),
                    inc=(s_pe if last else None))
            pe_end_bank(bank)
            ring_used(slot)
            consume_bank(bank, "scalar")
            ap_ = oc % 2
            R.wait("scalar", ac_sem[ap_], ac_sem[ap_].n)
            R.op("scalar", lambda e, b=bank, oc=oc, ap_=ap_: e.activation(
                accst[ap_][:, :], pb[b][:, :], AF.Identity,
                bias=uvb_sb[:, 16 + oc:16 + oc + 1], scale=cvals[:, 5:6]), inc=s_act)
            consumed_bank(bank, s_act)
            join("gpsimd", s_act, d_st)
            R.op("gpsimd", lambda e, oc=oc, cols=cols, ap_=ap_: e.dma_start(
                yout[oc * 128:(oc + 1) * 128, cols], accst[ap_][:, :],
                accum_op=ALU.add), inc=ac_sem[ap_])

    def ff1v(oc, ts):
        if oc < 16:
            return mid[oc * 2 + ts][:, :]
        if oc < 24:
            t = bdq[oc - 16].ap().bitcast(BF16)
            return t[:, ts * 512:(ts + 1) * 512]
        return hid2[oc - 24][:, ts * 512:(ts + 1) * 512]

    def emit_ff():
        for th in range(2):
            base = th * 1024
            for oc in range(32):
                slot = wload_cols(Wff1, oc * 128, 128, D, FFD)
                for ts in range(2):
                    cols = slice(base + ts * 512, base + (ts + 1) * 512)
                    bank = (oc * 2 + ts) % 2
                    proj_group(bank, slot, a1, cols)
                    consume_bank(bank, "scalar")
                    join("scalar", s_pe)
                    R.op("scalar", lambda e, b=bank, oc=oc, ts=ts: e.activation(
                        ff1v(oc, ts), pb[b][:, :], AF.Relu,
                        bias=bff1_sb[:, oc:oc + 1], scale=cvals[:, 5:6]), inc=s_act)
                    consumed_bank(bank, s_act)
            for dc in range(NCH):
                slots = [wload_cols(Wff2, dc * 128, 128, 1024, D, rowoff=rg * 1024)
                         for rg in range(4)]
                for ts in range(2):
                    cols = slice(base + ts * 512, base + (ts + 1) * 512)
                    bank = 2 + ts
                    pe_begin_bank(bank)
                    for rg in range(4):
                        R.wait("tensor", w_sem[slots[rg]], ring_state[slots[rg]]["loaded"])
                    join("tensor", s_act)
                    for oc in range(32):
                        last = oc == 31
                        sl = slots[oc // 8]
                        R.op("tensor", lambda e, b=bank, sl=sl, oc=oc, ts=ts, last=last: e.matmul(
                            pb[b][:, :], wring[sl][:, (oc % 8) * 128:(oc % 8 + 1) * 128],
                            ff1v(oc, ts), start=(oc == 0), stop=last),
                            inc=(s_pe if last else None))
                    pe_end_bank(bank)
                    for rg in range(4):
                        ring_used(slots[rg])
                    consume_bank(bank, "scalar")
                    join("scalar", s_dve)
                    qp = ts
                    R.op("scalar", lambda e, b=bank, dc=dc, qp=qp: e.activation(
                        qstage[qp][:, :], pb[b][:, :], AF.Identity,
                        bias=uvb_sb[:, 24 + dc:24 + dc + 1], scale=cvals[:, 5:6]), inc=s_act)
                    consumed_bank(bank, s_act)
                    join("vector", s_act)
                    R.op("vector", lambda e, dc=dc, cols=cols, qp=qp: e.tensor_add(
                        a1[dc][:, cols], qstage[qp][:, :], a1[dc][:, cols]), inc=s_dve)
            for ts in range(2):
                emit_ln_inplace(slice(base + ts * 512, base + (ts + 1) * 512), 16, 24)

    rt_pe_mark = [0]

    def emit_routing():
        t_rt = None
        if "t_rt" in taps:
            t_rt = nc.dram_tensor("t_rt", [128, 256], F32, kind="ExternalOutput").ap()
        for tt in range(16):
            bank = tt % 2
            pe_begin_bank(bank)
            join("tensor", s_dve, s_act)
            for fc in range(NCH):
                last = fc == NCH - 1
                R.op("tensor", lambda e, b=bank, fc=fc, tt=tt, last=last: e.matmul(
                    pb[b][0:128, 0:8], a1[fc][:, tt * 128:(tt + 1) * 128],
                    wg_sb[:, fc * 8:(fc + 1) * 8], start=(fc == 0), stop=last,
                    skip_group_check=True), inc=(s_pe if last else None))
            pe_end_bank(bank)
            consume_bank(bank, "vector")
            join("vector", s_act, d_st)
            R.wait("vector", s_pe, rt_pe_mark[0])
            R.op("vector", lambda e, b=bank: e.tensor_copy(rt[:, 0:8], pb[b][:, 0:8]), inc=s_dve)
            consumed_bank(bank, s_dve)
            vsync()
            R.op("vector", lambda e: e.reduce_max(rt[:, 8:9], rt[:, 0:8], axis=AX.X, negate=True), inc=s_dve)
            vsync()
            R.op("vector", lambda e: e.reduce_max(rt[:, 13:14], rt[:, 0:8], axis=AX.X), inc=s_dve)
            join("scalar", s_dve)
            R.op("scalar", lambda e: e.activation(rt[:, 16:24], rt[:, 0:8], AF.Exp,
                                                  bias=rt[:, 8:9], scale=cvals[:, 5:6]), inc=s_act)
            ssync()
            R.op("scalar", lambda e: e.activation(rt[:, 24:32], rt[:, 0:8], AF.Identity,
                                                  bias=rt[:, 13:14], scale=cvals[:, 6:7]), inc=s_act)
            join("vector", s_act)
            R.op("vector", lambda e: e.tensor_tensor(rt[:, 32:40], rt[:, 0:8], rt[:, 24:32], ALU.is_equal), inc=s_dve)
            vsync()
            R.op("vector", lambda e: e.reduce_sum(rt[:, 9:10], rt[:, 16:24], axis=AX.X), inc=s_dve)
            vsync()
            R.op("vector", lambda e: e.reciprocal(rt[:, 10:11], rt[:, 9:10]), inc=s_dve)
            vsync()
            R.op("vector", lambda e: e.scalar_tensor_tensor(
                rt[:, 40:48], rt[:, 32:40], 16384.0, rt[:, 56:64], ALU.mult, ALU.add), inc=s_dve)
            vsync()
            R.op("vector", lambda e: e.reduce_max(rt[:, 11:12], rt[:, 40:48], axis=AX.X), inc=s_dve)
            join("scalar", s_dve)
            R.op("scalar", lambda e: e.activation(rt[:, 40:48], rt[:, 48:56], AF.Square,
                                                  bias=rt[:, 11:12], scale=cvals[:, 5:6]), inc=s_act)
            ssync()
            R.op("scalar", lambda e: e.activation(rt[:, 32:40], rt[:, 40:48], AF.Relu,
                                                  bias=cvals[:, 5:6], scale=cvals[:, 4:5]), inc=s_act)
            ssync()
            R.op("scalar", lambda e: e.activation(rt[:, 24:32], rt[:, 32:40], AF.Identity,
                                                  bias=cvals[:, 6:7], scale=rt[:, 10:11]), inc=s_act)
            if t_rt is not None:
                join("sync", s_act, s_dve)
                R.op("sync", lambda e, tt=tt, t_rt=t_rt: e.dma_start(t_rt[:, tt * 16:(tt + 1) * 16], rt[:, 0:16]), inc=d_st)
            pe_begin_bank(5)
            join("tensor", s_act)
            R.op("tensor", lambda e: e.matmul(pb[5][0:8, 0:128], rt[:, 24:32], id_sb[:, :],
                                              start=True, stop=True, is_transpose=True), inc=s_pe)
            pe_end_bank(5)
            rt_pe_mark[0] = s_pe.n
            consume_bank(5, "scalar")
            join("scalar", s_dve)
            R.op("scalar", lambda e, tt=tt: e.copy(wT_bf[0:8, tt * 128:(tt + 1) * 128], pb[5][0:8, 0:128]), inc=s_act)
            consumed_bank(5, s_act)

    def emit_moe():
        first_th = True
        for th in range(2):
            base = th * 1024
            for e_ in range(NE):
                for oc in range(32):
                    slot = wload_cols(We1, e_ * D * FFD + oc * 128, 128, D, FFD)
                    for ts in range(2):
                        cols = slice(base + ts * 512, base + (ts + 1) * 512)
                        bank = (oc * 2 + ts) % 2
                        proj_group(bank, slot, a1, cols)
                        consume_bank(bank, "scalar")
                        join("scalar", s_pe)
                        R.op("scalar", lambda e, b=bank, e_=e_, oc=oc, ts=ts: e.activation(
                            ff1v(oc, ts), pb[b][:, :], AF.Relu,
                            bias=be1_sb[:, e_ * 32 + oc:e_ * 32 + oc + 1], scale=cvals[:, 5:6]), inc=s_act)
                        consumed_bank(bank, s_act)
                for ts in range(2):
                    pe_begin_bank(5)
                    join("tensor", s_act)
                    R.op("tensor", lambda e, e_=e_, base=base, ts=ts: e.matmul(
                        pb[5][:, :], eye8_sb[:, e_ * 128:(e_ + 1) * 128],
                        wT_bf[0:8, base + ts * 512:base + (ts + 1) * 512],
                        start=True, stop=True), inc=s_pe)
                    pe_end_bank(5)
                    consume_bank(5, "scalar")
                    join("scalar", s_dve)
                    R.op("scalar", lambda e, ts=ts: e.copy(wbc_sb[:, ts * 512:(ts + 1) * 512], pb[5][:, :]), inc=s_act)
                    consumed_bank(5, s_act)
                if first_th and e_ == 0 and "t_wbc" in taps:
                    t_wbc = nc.dram_tensor("t_wbc", [128, 1024], BF16, kind="ExternalOutput").ap()
                    join("sync", s_act)
                    R.op("sync", lambda e, t=t_wbc: e.dma_start(t[:, :], wbc_sb[:, :]), inc=d_st)
                    join("gpsimd", d_st)
                for dc in range(NCH):
                    slots = [wload_cols(We2, e_ * FFD * D + dc * 128, 128, 1024, D,
                                        rowoff=rg * 1024) for rg in range(4)]
                    for ts in range(2):
                        bank = 2 + ts
                        pe_begin_bank(bank)
                        for rg in range(4):
                            R.wait("tensor", w_sem[slots[rg]], ring_state[slots[rg]]["loaded"])
                        join("tensor", s_act)
                        for oc in range(32):
                            last = oc == 31
                            sl = slots[oc // 8]
                            R.op("tensor", lambda e, b=bank, sl=sl, oc=oc, ts=ts, last=last: e.matmul(
                                pb[b][:, :], wring[sl][:, (oc % 8) * 128:(oc % 8 + 1) * 128],
                                ff1v(oc, ts), start=(oc == 0), stop=last),
                                inc=(s_pe if last else None))
                        pe_end_bank(bank)
                        for rg in range(4):
                            ring_used(slots[rg])
                        consume_bank(bank, "scalar")
                        join("scalar", s_dve)
                        qp = ts
                        R.op("scalar", lambda e, b=bank, dc=dc, e_=e_, qp=qp: e.activation(
                            qstage[qp][:, :], pb[b][:, :], AF.Identity,
                            bias=be2_sb[:, e_ * 8 + dc:e_ * 8 + dc + 1], scale=cvals[:, 5:6]), inc=s_act)
                        consumed_bank(bank, s_act)
                        join("vector", s_act)
                        pc = slice(ts * 512, (ts + 1) * 512)
                        if first_th and e_ == 0 and dc == 0 and ts == 0 and "t_wbc" in taps:
                            t_qs = nc.dram_tensor("t_qs", [128, 512], BF16, kind="ExternalOutput").ap()
                            join("sync", s_act)
                            R.op("sync", lambda e, t=t_qs, qp=qp: e.dma_start(t[:, :], qstage[qp][:, :]), inc=d_st)
                            join("vector", d_st)
                        if e_ == 0:
                            R.op("vector", lambda e, dc=dc, pc=pc, ts=ts, qp=qp: e.tensor_mul(
                                pacc[dc][:, pc], qstage[qp][:, :],
                                wbc_sb[:, ts * 512:(ts + 1) * 512]), inc=s_dve)
                        else:
                            R.op("vector", lambda e, ts=ts, qp=qp: e.tensor_mul(
                                rtmp[0][:, :], qstage[qp][:, :],
                                wbc_sb[:, ts * 512:(ts + 1) * 512]))
                            R.op("vector", lambda e, dc=dc, pc=pc: e.tensor_add(
                                pacc[dc][:, pc], pacc[dc][:, pc], rtmp[0][:, :]), inc=s_dve)
            for ts in range(2):
                cols = slice(base + ts * 512, base + (ts + 1) * 512)
                pc = slice(ts * 512, (ts + 1) * 512)
                for dc in range(NCH):
                    R.op("vector", lambda e, dc=dc, pc=pc, cols=cols: e.tensor_add(
                        pacc[dc][:, pc], pacc[dc][:, pc], a1[dc][:, cols]), inc=s_dve)
                join("scalar", s_dve)
                for c in range(NCH):
                    ln_stats_chunk(pacc[c][:, pc], c, sqst[c % 2][:, :])
                rv = ln_rows()
                ln_bcast(rv)
                for c in range(NCH):
                    consume_bank(6, "vector")
                    consume_bank(7, "vector")
                    join("vector", s_act)
                    R.op("vector", lambda e, c=c, pc=pc: e.tensor_mul(ttmp[:, :], pacc[c][:, pc], pb[6][:, :]))
                    R.op("vector", lambda e: e.tensor_sub(xstage[0][:, :], ttmp[:, :], pb[7][:, :]), inc=s_dve)
                    join("scalar", s_dve)
                    R.op("scalar", lambda e, c=c, cols=cols: e.activation(
                        a1[c][:, cols], xstage[0][:, :], AF.Identity,
                        bias=lnp_sb[:, 40 + c:40 + c + 1], scale=lnp_sb[:, 32 + c:32 + c + 1]), inc=s_act)
                consumed_bank(6, s_dve)
                consumed_bank(7, s_dve)
                for oc in range(NCH):
                    slot = wload_cols(Wout, oc * 128, 128, D, D)
                    bank = oc % 2
                    proj_group(bank, slot, a1, cols)
                    consume_bank(bank, "scalar")
                    ap_ = oc % 2
                    R.wait("scalar", ac_sem[ap_], ac_sem[ap_].n)
                    R.op("scalar", lambda e, b=bank, ap_=ap_: e.copy(accst[ap_][:, :], pb[b][:, :]), inc=s_act)
                    consumed_bank(bank, s_act)
                    join("gpsimd", s_act, d_st)
                    R.op("gpsimd", lambda e, oc=oc, cols=cols, ap_=ap_: e.dma_start(
                        yout[oc * 128:(oc + 1) * 128, cols], accst[ap_][:, :],
                        accum_op=ALU.add), inc=ac_sem[ap_])
            if first_th and "t_p3a" in taps:
                tap_pacc("t_p3a")
            first_th = False

    def tap_a1(name):
        t = nc.dram_tensor(name, [D, S], F32, kind="ExternalOutput").ap()
        join("vector", s_act, s_pe)
        for c in range(NCH):
            for tc in range(NTC):
                join("vector", d_st)
                R.op("vector", lambda e, c=c, tc=tc: e.tensor_copy(
                    ttmp[:, :], a1[c][:, tc * 512:(tc + 1) * 512]), inc=s_dve)
                join("sync", s_dve)
                R.op("sync", lambda e, t=t, c=c, tc=tc: e.dma_start(
                    bass.AP(t.tensor, c * 128 * S + tc * 512, [[S, 128], [1, 512]]),
                    ttmp[:, :]), inc=d_st)

    def tap_y(name):
        t = nc.dram_tensor(name, [D, S], F32, kind="ExternalOutput").ap()
        join("sync", ac_sem[0], ac_sem[1], d_st, s_act, s_dve)
        for c in range(NCH):
            for tc in range(NTC):
                p = (c + tc) % 2
                R.op("sync", lambda e, c=c, tc=tc, p=p: e.dma_start(
                    xstage[p][:, :], yout[c * 128:(c + 1) * 128, tc * 512:(tc + 1) * 512]), inc=d_ld)
                join("sync", d_ld)
                R.op("sync", lambda e, t=t, c=c, tc=tc, p=p: e.dma_start(
                    bass.AP(t.tensor, c * 128 * S + tc * 512, [[S, 128], [1, 512]]),
                    xstage[p][:, :]), inc=d_st)

    def tap_hctx(name):
        t = nc.dram_tensor(name, [D, 512], F32, kind="ExternalOutput").ap()
        join("vector", s_act, s_pe, s_dve)
        for c in range(NCH):
            join("vector", d_st)
            R.op("vector", lambda e, c=c: e.tensor_copy(ttmp[:, :], hctx[c][:, :]), inc=s_dve)
            join("sync", s_dve)
            R.op("sync", lambda e, t=t, c=c: e.dma_start(
                bass.AP(t.tensor, c * 128 * 512, [[512, 128], [1, 512]]), ttmp[:, :]), inc=d_st)

    def tap_wt(name):
        t = nc.dram_tensor(name, [NE, S], BF16, kind="ExternalOutput").ap()
        join("sync", s_act, s_dve)
        R.op("sync", lambda e, t=t: e.dma_start(t[:, :], wT_bf[:, :]), inc=d_st)

    def tap_pacc(name):
        t = nc.dram_tensor(name, [D, 1024], BF16, kind="ExternalOutput").ap()
        join("sync", s_dve, s_act)
        for c in range(NCH):
            R.op("sync", lambda e, t=t, c=c: e.dma_start(
                t[c * 128:(c + 1) * 128, :], pacc[c][:, :]), inc=d_st)

    join("tensor", s_pool)

    def ffn_call(first):
        emit_ln0()
        if first and "t_h" in taps:
            tap_a1("t_h")
        emit_qkv_txl()
        emit_qkv_v()
        emit_attention(True, ffn_epilogue)
        if first and "t_ctx" in taps:
            tap_hctx("t_ctx")
        if first and "t_h1" in taps:
            tap_a1("t_h1")
        emit_ff()
        if first and "t_h2" in taps:
            tap_a1("t_h2")
        emit_routing()
        if first and "t_wt" in taps:
            tap_wt("t_wt")
        emit_moe()
        if first and "t_p3" in taps:
            tap_pacc("t_p3")
        if first and "t_h3" in taps:
            tap_a1("t_h3")

    def attn_call():
        emit_ln0()
        emit_qkv_rope()
        emit_qkv_v(WsrcT=WvT)
        emit_attention(False, attn_epilogue)

    ffn_call(True)
    if "t_x1" in taps:
        tap_y("t_x1")
    if stages >= 2:
        attn_call()
        if "t_x2" in taps:
            tap_y("t_x2")
    if stages >= 3:
        ffn_call(False)

    join("sync", s_dve, s_act, s_pe, s_pool)
    for s_ in (d_st, d_ld, d_mw, d_mr, d_w, d_acc, d_mid, rts,
               *w_sem, *hl_sem, *vo_sem, *xs_sem, *mr_sem, *qs_sem, *ac_sem):
        R.wait("sync", s_)

    with nc.Block() as block:
        R.replay(block)
    stck.close()
    return nc


def kernel(**inputs):
    shared = _prep_shared(inputs)
    x = np.asarray(inputs["L0_attn_x"])
    nc = _build()
    in_maps = []
    for b in range(8):
        m = dict(shared)
        m["xT"] = np.ascontiguousarray(x[b].T)
        in_maps.append(m)
    res = run_bass_kernel_spmd(nc, in_maps, core_ids=list(range(8)))
    out = np.stack([np.asarray(r["y"]).T for r in res.results], axis=0)
    return out.astype(np.float32)


# version 06
# speedup vs baseline: 1.0896x; 1.0762x over previous
import numpy as np
import ml_dtypes
from contextlib import ExitStack

import concourse.bass as bass
import concourse.mybir as mybir
from concourse.bass_utils import run_bass_kernel_spmd

F32 = mybir.dt.float32
BF16 = mybir.dt.bfloat16
AF = mybir.ActivationFunctionType
ALU = mybir.AluOpType
AX = mybir.AxisListType
BF = ml_dtypes.bfloat16

S = 2048
D = 1024
H = 16
HD = 64
FFD = 4096
NE = 8
NCH = 8
NTC = 4
MW = 2560
EPS = 1e-5

ENGS = ("tensor", "vector", "scalar", "gpsimd", "sync")


class Sem:
    def __init__(self, h, step=1):
        self.h = h
        self.step = step
        self.n = 0


class Rec:
    def __init__(self):
        self.q = {e: [] for e in ENGS}
        self.waited = {e: {} for e in ENGS}

    def op(self, eng, fn, inc=None):
        if inc is not None:
            inc.n += inc.step
            h, stp = inc.h, inc.step
            self.q[eng].append(lambda e, fn=fn, h=h, stp=stp: fn(e).then_inc(h, stp))
            return inc.n
        self.q[eng].append(lambda e, fn=fn: fn(e))
        return None

    def wait(self, eng, sem, val=None):
        v = sem.n if val is None else val
        if v <= 0:
            return
        if self.waited[eng].get(id(sem), -1) >= v:
            return
        self.waited[eng][id(sem)] = v
        self.q[eng].append(lambda e, h=sem.h, v=v: e.wait_ge(h, v))

    def replay(self, block):
        rq = self.q

        @block.tensor
        def _(e):
            for f in rq["tensor"]:
                f(e)

        @block.vector
        def _(e):
            for f in rq["vector"]:
                f(e)

        @block.scalar
        def _(e):
            for f in rq["scalar"]:
                f(e)

        @block.gpsimd
        def _(e):
            for f in rq["gpsimd"]:
                f(e)

        @block.sync
        def _(e):
            for f in rq["sync"]:
                f(e)


def _host_constants():
    c = {}
    pos = np.arange(S - 1, -1, -1, dtype=np.float64)
    inv_freq = 1.0 / 10000.0 ** (np.arange(0, D, 2, dtype=np.float64) / D)
    sinus = pos[:, None] * inv_freq[None, :]
    pe = np.concatenate([np.sin(sinus), np.cos(sinus)], axis=-1)
    c["c_pet"] = np.ascontiguousarray(pe.T).astype(np.float32)
    ang = np.arange(S, dtype=np.float64)[:, None] * (
        1.0 / 10000.0 ** (np.arange(0, HD, 2, dtype=np.float64) / HD))[None, :]
    cos64 = np.repeat(np.cos(ang), 2, axis=1).T
    sin64 = np.repeat(np.sin(ang), 2, axis=1).T
    sgn = np.where(np.arange(HD) % 2 == 0, -1.0, 1.0)[:, None]
    sinm64 = sin64 * sgn
    c["c_cos"] = np.vstack([cos64, cos64]).astype(BF)
    c["c_sinm"] = np.vstack([sinm64, sinm64]).astype(BF)
    c["c_id"] = np.eye(128, dtype=np.float32)
    pswap = np.zeros((128, 128), dtype=np.float32)
    for i in range(0, 128, 2):
        pswap[i + 1, i] = 1.0
        pswap[i, i + 1] = 1.0
    c["c_pswap"] = pswap.astype(BF)
    eye8 = np.zeros((NE, NE * 128), dtype=np.float32)
    for e in range(NE):
        eye8[e, e * 128:(e + 1) * 128] = 1.0
    c["c_eye8"] = eye8.astype(BF)
    return c


def _col8(v):
    return np.ascontiguousarray(v.reshape(8, 128).T)


def _prep_shared(w):
    m = {}
    m.update(_host_constants())
    m["WqT"] = np.ascontiguousarray(np.asarray(w["L0_attn_Wq"]).T)
    m["WkT"] = np.ascontiguousarray(np.asarray(w["L0_attn_Wk"]).T)
    m["WvT"] = np.ascontiguousarray(np.asarray(w["L0_attn_Wv"]).T)
    m["WoaT"] = np.ascontiguousarray(np.asarray(w["L0_attn_Wo"]).T)
    m["boc"] = _col8(np.asarray(w["L0_attn_bo"]))
    for k in ("Wqkv", "Wo", "Wr", "Wff1", "Wff2", "Wg", "We1", "We2", "Wout"):
        m[k] = np.ascontiguousarray(np.asarray(w["L0_ffn_" + k]))
    m["uc"] = _col8(np.asarray(w["L0_ffn_u_bias"]).reshape(-1))
    m["vc"] = _col8(np.asarray(w["L0_ffn_v_bias"]).reshape(-1))
    for i in (1, 2, 3):
        m[f"ln{i}g"] = _col8(np.asarray(w[f"L0_ffn_ln{i}_g"]))
        m[f"ln{i}b"] = _col8(np.asarray(w[f"L0_ffn_ln{i}_b"]))
    m["bff1c"] = np.ascontiguousarray(np.asarray(w["L0_ffn_bff1"]).reshape(32, 128).T)
    m["bff2c"] = _col8(np.asarray(w["L0_ffn_bff2"]))
    be1 = np.asarray(w["L0_ffn_be1"])
    m["be1c"] = np.ascontiguousarray(
        be1.reshape(NE, 32, 128).transpose(2, 0, 1).reshape(128, NE * 32))
    be2 = np.asarray(w["L0_ffn_be2"])
    m["be2c"] = np.ascontiguousarray(
        be2.reshape(NE, 8, 128).transpose(2, 0, 1).reshape(128, NE * 8))
    m["be2r"] = np.ascontiguousarray(be2)
    return m


def _build(taps=(), stages=3):
    taps = set(taps)
    nc = bass.Bass("TRN2", target_bir_lowering=False)

    def dram_in(name, shape, dtype=F32):
        return nc.dram_tensor(name, list(shape), dtype, kind="ExternalInput").ap()

    def dram_int(name, shape, dtype=F32):
        kind = "ExternalOutput" if name in taps else "Internal"
        return nc.dram_tensor(name, list(shape), dtype, kind=kind).ap()

    xin = dram_in("xT", (D, S))
    WqT = dram_in("WqT", (D, D)); WkT = dram_in("WkT", (D, D))
    WvT = dram_in("WvT", (D, D)); WoaT = dram_in("WoaT", (D, D))
    Wqkv = dram_in("Wqkv", (D, 3 * D)); Wo = dram_in("Wo", (D, D))
    Wr = dram_in("Wr", (D, D)); Wff1 = dram_in("Wff1", (D, FFD))
    Wff2 = dram_in("Wff2", (FFD, D)); Wg = dram_in("Wg", (D, NE))
    We1 = dram_in("We1", (NE, D, FFD)); We2 = dram_in("We2", (NE, FFD, D))
    Wout = dram_in("Wout", (D, D))
    boc = dram_in("boc", (128, 8)); uc = dram_in("uc", (128, 8))
    vc = dram_in("vc", (128, 8))
    lng = {i: dram_in(f"ln{i}g", (128, 8)) for i in (1, 2, 3)}
    lnb = {i: dram_in(f"ln{i}b", (128, 8)) for i in (1, 2, 3)}
    bff1c = dram_in("bff1c", (128, 32))
    bff2c = dram_in("bff2c", (128, 8))
    be1c = dram_in("be1c", (128, NE * 32))
    be2c = dram_in("be2c", (128, NE * 8))
    be2r = dram_in("be2r", (NE, D))
    c_pet = dram_in("c_pet", (D, S))
    c_cos = dram_in("c_cos", (128, S), BF16)
    c_sinm = dram_in("c_sinm", (128, S), BF16)
    c_id = dram_in("c_id", (128, 128))
    c_pswap = dram_in("c_pswap", (128, 128), BF16)
    c_eye8 = dram_in("c_eye8", (NE, NE * 128), BF16)

    yout = nc.dram_tensor("y", [D, S], F32, kind="ExternalOutput").ap()

    r_d = dram_int("r_d", (D, S), BF16)
    qu_d = dram_int("qu_d", (D, S), BF16)
    qv_d = dram_int("qv_d", (D, S), BF16)
    kk_d = dram_int("kk_d", (D, S), BF16)
    v_d = dram_int("v_d", (S, D), BF16)
    M_d = nc.dram_tensor("M_d", [2, 512, MW], F32, kind="Internal").ap()

    A = nc.alloc_sbuf_tensor
    a1 = [A(f"a1_{c}", [128, S], BF16) for c in range(NCH)]
    mid = [A(f"mid{i}", [128, 512], BF16) for i in range(32)]
    bdq = [A(f"bdq_{i}", [128, 512], F32) for i in range(8)]
    pacc = [A(f"pacc{i}", [128, 1024], BF16) for i in range(NE)]
    hid2 = [A(f"hid2_{i}", [128, 1024], BF16) for i in range(8)]
    hctx = [A(f"hctx{c}", [128, 512], BF16) for c in range(NCH)]
    wring = [A(f"wr{i}", [128, 1024], BF16) for i in range(6)]
    kr_b = [A(f"krb{i}", [128, S], BF16) for i in range(3)]
    xstage = [A(f"xst{i}", [128, 512], F32) for i in range(2)]
    cos_sb = A("cos_sb", [128, S], BF16)
    sinm_sb = A("sinm_sb", [128, S], BF16)
    voh_b = [A(f"voh{i}", [128, H * 65], BF16) for i in range(2)]
    mstage = [A(f"mst{i}", [128, 512], F32) for i in range(2)]
    lnrow = A("lnrow", [128, 2048], F32)
    wT_bf = A("wT_bf", [NE, S], BF16)
    quv_b = [A(f"quvb{i}", [128, 512], BF16) for i in range(3)]
    accst = [A(f"acc{i}", [128, 512], F32) for i in range(2)]
    ptile = [A(f"pt{i}", [128, 512], BF16) for i in range(3)]
    qstage = [A(f"qst{i}", [128, 512], BF16) for i in range(2)]
    rtmp = [A(f"rtmp{i}", [128, 512], BF16) for i in range(2)]
    ttmp = A("ttmp", [128, 512], F32)
    sqst = [A(f"sqst{i}", [128, 512], BF16) for i in range(2)]
    eye8_sb = A("eye8_sb", [NE, NE * 128], BF16)
    be2r_sb = A("be2r_sb", [NE, D], BF16)
    wbc_sb = A("wbc_sb", [128, 1024], BF16)
    recip_row = A("recip_row", [1, 512], F32)
    bcsb = A("bcsb", [64, 512], BF16)
    id_sb = A("id_sb", [128, 128], F32)
    ones_f = A("ones_f", [1, 128], F32)
    ones_bf = A("ones_bf", [1, 128], BF16)
    onesc_bf = A("onesc_bf", [128, 1], BF16)
    pswap_sb = A("pswap_sb", [128, 128], BF16)
    uvb_sb = A("uvb_sb", [128, 32], F32)
    lnp_sb = A("lnp_sb", [128, 48], F32)
    bff1_sb = A("bff1_sb", [128, 32], F32)
    be1_sb = A("be1_sb", [128, NE * 32], F32)
    be2_sb = A("be2_sb", [128, NE * 8], F32)
    wg_sb = A("wg_sb", [128, 64], BF16)
    rt = A("rt", [128, 64], F32)
    cvals = A("cvals", [128, 8], F32)

    pb = [nc.alloc_psum_tensor(f"pb{i}", [128, 512], F32) for i in range(8)]

    stck = ExitStack()
    sems = {}
    names = [("s_pe", 1), ("s_act", 1), ("s_dve", 1), ("s_pool", 1),
             ("d_w", 16), ("d_acc", 16), ("d_ld", 16), ("d_st", 16),
             ("d_mw", 16), ("d_mr", 16), ("d_mid", 16)]
    names += [(f"w{i}", 16) for i in range(6)]
    names += [(f"hl{i}", 16) for i in range(3)]
    names += [(f"vo{i}", 16) for i in range(2)]
    names += [(f"xs{i}", 16) for i in range(2)]
    names += [(f"mr{i}", 16) for i in range(2)]
    names += [(f"qs{i}", 16) for i in range(2)]
    names += [("rts", 16)]
    names += [(f"ac{i}", 16) for i in range(2)]
    for name, step in names:
        sems[name] = Sem(stck.enter_context(nc.semaphore(name)), step)
    s_pe, s_act, s_dve, s_pool = (sems[k] for k in ("s_pe", "s_act", "s_dve", "s_pool"))
    d_w, d_acc, d_ld, d_st, d_mw, d_mr = (
        sems[k] for k in ("d_w", "d_acc", "d_ld", "d_st", "d_mw", "d_mr"))
    d_mid = sems["d_mid"]
    w_sem = [sems[f"w{i}"] for i in range(6)]
    hl_sem = [sems[f"hl{i}"] for i in range(3)]
    vo_sem = [sems[f"vo{i}"] for i in range(2)]
    xs_sem = [sems[f"xs{i}"] for i in range(2)]
    mr_sem = [sems[f"mr{i}"] for i in range(2)]
    qs_sem = [sems[f"qs{i}"] for i in range(2)]
    rts = sems["rts"]
    ac_sem = [sems[f"ac{i}"] for i in range(2)]

    R = Rec()
    bank_free = [None] * 8
    bank_prod = [None] * 8

    def pe_begin_bank(i):
        if bank_free[i] is not None:
            R.wait("tensor", bank_free[i][0], bank_free[i][1])

    def pe_end_bank(i):
        bank_prod[i] = (s_pe, s_pe.n)

    def consume_bank(i, eng):
        R.wait(eng, bank_prod[i][0], bank_prod[i][1])

    def consumed_bank(i, sem):
        bank_free[i] = (sem, sem.n)

    def join(eng, *ss):
        for s in ss:
            R.wait(eng, s)

    ring_state = [{"loaded": 0, "lastuse": 0} for _ in wring]
    ring_ptr = [0]

    def wload_cols(dram, coloff, ncols, nrows, rowstride, rowoff=0):
        slot = ring_ptr[0]
        ring_ptr[0] = (slot + 1) % len(wring)
        stt = ring_state[slot]
        if stt["lastuse"]:
            R.wait("gpsimd", s_pe, stt["lastuse"])
        nfc = nrows // 128
        src = bass.AP(dram.tensor, rowoff * rowstride + coloff,
                      [[rowstride, 128], [128 * rowstride, nfc], [1, ncols]])
        R.op("gpsimd", lambda e, d=wring[slot][:, 0:nfc * ncols], s=src: e.dma_start(d, s), inc=w_sem[slot])
        stt["loaded"] = w_sem[slot].n
        return slot

    def ring_used(slot):
        ring_state[slot]["lastuse"] = s_pe.n

    for c in range(NCH):
        for tc in range(NTC):
            p = (c * NTC + tc) % 2
            join("sync", d_st)
            R.op("sync", lambda e, c=c, tc=tc, p=p: e.dma_start(
                accst[p][:, :], xin[c * 128:(c + 1) * 128, tc * 512:(tc + 1) * 512]), inc=d_ld)
            join("sync", d_ld)
            R.op("sync", lambda e, c=c, tc=tc, p=p: e.dma_start(
                yout[c * 128:(c + 1) * 128, tc * 512:(tc + 1) * 512], accst[p][:, :]), inc=d_st)
    for dst, src in ((cos_sb, c_cos), (sinm_sb, c_sinm), (id_sb, c_id),
                     (pswap_sb, c_pswap), (bff1_sb, bff1c), (be1_sb, be1c),
                     (be2_sb, be2c), (eye8_sb, c_eye8)):
        R.op("sync", lambda e, d=dst, s=src: e.dma_start(d[:, :], s[:, :]), inc=d_ld)
    R.op("gpsimd", lambda e: e.dma_start(be2r_sb[:, :], be2r[:, :]), inc=d_mid)
    wg_src = bass.AP(Wg.tensor, 0, [[NE, 128], [128 * NE, 8], [1, NE]])
    R.op("gpsimd", lambda e: e.dma_start(wg_sb[:, :], wg_src), inc=d_mid)
    R.op("sync", lambda e: e.dma_start(uvb_sb[:, 0:8], uc[:, :]), inc=d_ld)
    R.op("sync", lambda e: e.dma_start(uvb_sb[:, 8:16], vc[:, :]), inc=d_ld)
    R.op("sync", lambda e: e.dma_start(uvb_sb[:, 16:24], boc[:, :]), inc=d_ld)
    R.op("sync", lambda e: e.dma_start(uvb_sb[:, 24:32], bff2c[:, :]), inc=d_ld)
    for i, k in ((0, 1), (2, 2), (4, 3)):
        R.op("sync", lambda e, i=i, k=k: e.dma_start(lnp_sb[:, i * 8:(i + 1) * 8], lng[k][:, :]), inc=d_ld)
        R.op("sync", lambda e, i=i, k=k: e.dma_start(lnp_sb[:, (i + 1) * 8:(i + 2) * 8], lnb[k][:, :]), inc=d_ld)
    join("gpsimd", d_ld)
    for i, v in ((0, EPS), (1, -0.5), (2, 0.125), (3, 1.0 / 64), (4, -1.0),
                 (5, 1.0), (6, 0.0)):
        R.op("gpsimd", lambda e, i=i, v=v: e.memset(cvals[:, i:i + 1], v))
    R.op("gpsimd", lambda e: e.memset(ones_f[:, :], 1.0))
    R.op("gpsimd", lambda e: e.memset(ones_bf[:, :], 1.0))
    R.op("gpsimd", lambda e: e.memset(onesc_bf[:, :], 1.0))
    R.op("gpsimd", lambda e: e.iota(rt[:, 48:56], [[1, NE]], channel_multiplier=0,
                                    allow_small_or_imprecise_dtypes=True), inc=s_pool)
    R.wait("gpsimd", s_pool, s_pool.n)
    R.op("gpsimd", lambda e: e.tensor_scalar(rt[:, 56:64], rt[:, 48:56], -1.0, -16384.0, ALU.mult, ALU.add))
    R.op("gpsimd", lambda e: e.tensor_scalar(uvb_sb[:, 0:16], uvb_sb[:, 0:16], 0.125, None, ALU.mult))
    R.op("gpsimd", lambda e: e.tensor_scalar(lnp_sb[:, 32:48], lnp_sb[:, 32:48], 0.5, None, ALU.mult))
    for b in range(2):
        R.op("gpsimd", lambda e, b=b: e.memset(voh_b[b][:, :], 1.0))
    R.op("gpsimd", lambda e: e.memset(mstage[0][:, :], 0.0), inc=s_pool)
    join("sync", s_pool)
    for reg in range(2):
        for rb_ in range(4):
            dst = bass.AP(M_d.tensor, reg * 512 * MW + rb_ * 128 * MW + 2048,
                          [[MW, 128], [1, 512]])
            R.op("sync", lambda e, d=dst: e.dma_start(d, mstage[0][:, :]), inc=d_st)

    for fc in range(NCH):
        for jc in range(4):
            R.op("gpsimd", lambda e, fc=fc, jc=jc: e.dma_start(
                mid[fc * 4 + jc][:, :],
                c_pet[fc * 128:(fc + 1) * 128, jc * 512:(jc + 1) * 512]), inc=d_mid)
    pe_loads = d_mid.n
    for oc in range(NCH):
        slot = wload_cols(Wr, oc * 128, 128, D, D)
        for jc in range(4):
            bank = (oc * 4 + jc) % 2
            pe_begin_bank(bank)
            R.wait("tensor", w_sem[slot], ring_state[slot]["loaded"])
            R.wait("tensor", d_mid, pe_loads)
            for fc in range(NCH):
                last = fc == NCH - 1
                R.op("tensor", lambda e, b=bank, s=slot, fc=fc, jc=jc, last=last: e.matmul(
                    pb[b][:, :], wring[s][:, fc * 128:(fc + 1) * 128],
                    mid[fc * 4 + jc][:, :], start=(fc == 0), stop=last),
                    inc=(s_pe if last else None))
            pe_end_bank(bank)
            ring_used(slot)
            consume_bank(bank, "scalar")
            q = (oc * 4 + jc) % 2
            R.wait("scalar", qs_sem[q], qs_sem[q].n)
            R.op("scalar", lambda e, b=bank, q=q: e.copy(qstage[q][:, :], pb[b][:, :]), inc=s_act)
            consumed_bank(bank, s_act)
            join("sync", s_act)
            R.op("sync", lambda e, oc=oc, jc=jc, q=q: e.dma_start(
                bass.AP(r_d.tensor, oc * 128 * S + jc * 512, [[S, 128], [1, 512]]),
                qstage[q][:, :]), inc=qs_sem[q])

    def lr(cb):
        return lnrow[0:1, cb:cb + 512]

    LR_M, LR_T, LR_B, LR_RSTD = 0, 512, 1024, 1536

    def vsync():
        R.wait("vector", s_dve, s_dve.n)

    def ssync():
        R.wait("scalar", s_act, s_act.n)

    def ln_rows():
        consume_bank(6, "vector")
        consume_bank(7, "vector")
        R.op("vector", lambda e: e.tensor_scalar(lr(LR_M), pb[6][0:1, :], 1.0 / D, None, ALU.mult), inc=s_dve)
        vsync()
        R.op("vector", lambda e: e.tensor_mul(lr(LR_T), lr(LR_M), lr(LR_M)), inc=s_dve)
        vsync()
        R.op("vector", lambda e: e.scalar_tensor_tensor(
            lr(LR_T), pb[7][0:1, :], 1.0 / D, lr(LR_T), ALU.mult, ALU.subtract), inc=s_dve)
        consumed_bank(6, s_dve)
        consumed_bank(7, s_dve)
        join("scalar", s_dve)
        R.op("scalar", lambda e: e.activation(lr(LR_B), lr(LR_T), AF.Ln, bias=cvals[0:1, 0:1], scale=cvals[0:1, 5:6]), inc=s_act)
        ssync()
        R.op("scalar", lambda e: e.activation(lr(LR_RSTD), lr(LR_B), AF.Exp, bias=cvals[0:1, 6:7], scale=cvals[0:1, 1:2]), inc=s_act)
        join("vector", s_act)
        R.op("vector", lambda e: e.tensor_mul(lr(LR_B), lr(LR_M), lr(LR_RSTD)), inc=s_dve)
        return s_dve.n

    def ln_bcast(rv):
        R.wait("tensor", s_dve, rv)
        pe_begin_bank(6)
        pe_begin_bank(7)
        R.op("tensor", lambda e: e.matmul(pb[6][:, :], ones_f[0:1, :], lr(LR_RSTD), start=True, stop=True))
        R.op("tensor", lambda e: e.matmul(pb[7][:, :], ones_f[0:1, :], lr(LR_B), start=True, stop=True), inc=s_pe)
        pe_end_bank(6)
        pe_end_bank(7)

    sq_mark = {0: 0, 1: 0}

    def ln_stats_chunk(src_bf, c, sqslot):
        R.wait("scalar", s_pe, sq_mark[c % 2])
        R.op("scalar", lambda e, s=src_bf, q=sqslot: e.activation(q, s, AF.Square), inc=s_act)
        if c == 0:
            pe_begin_bank(6)
            pe_begin_bank(7)
        last = c == NCH - 1
        join("tensor", s_act)
        R.op("tensor", lambda e, s=src_bf, c=c, last=last: e.matmul(
            pb[6][0:1, :], onesc_bf[:, 0:1], s, start=(c == 0), stop=last,
            skip_group_check=True))
        R.op("tensor", lambda e, q=sqslot, c=c, last=last: e.matmul(
            pb[7][0:1, :], onesc_bf[:, 0:1], q, start=(c == 0), stop=last,
            skip_group_check=True), inc=s_pe)
        sq_mark[c % 2] = s_pe.n
        if last:
            pe_end_bank(6)
            pe_end_bank(7)

    def emit_ln0():
        for tc in range(NTC):
            cols = slice(tc * 512, (tc + 1) * 512)
            join("gpsimd", ac_sem[0], ac_sem[1], d_st, s_pe, s_act)
            for c in range(NCH):
                R.op("gpsimd", lambda e, c=c, cols=cols: e.dma_start(
                    mid[c][:, :], yout[c * 128:(c + 1) * 128, cols]), inc=d_mid)
            join("scalar", d_mid)
            for c in range(NCH):
                ln_stats_chunk(mid[c][:, :], c, sqst[c % 2][:, :])
            rv = ln_rows()
            ln_bcast(rv)
            for c in range(NCH):
                p = c % 2
                join("sync", s_dve, ac_sem[0], ac_sem[1], d_st)
                R.op("sync", lambda e, c=c, cols=cols, p=p: e.dma_start(
                    xstage[p][:, :], yout[c * 128:(c + 1) * 128, cols]), inc=xs_sem[p])
                consume_bank(6, "vector")
                consume_bank(7, "vector")
                R.wait("vector", xs_sem[p], xs_sem[p].n)
                R.op("vector", lambda e, p=p: e.tensor_mul(ttmp[:, :], xstage[p][:, :], pb[6][:, :]))
                R.op("vector", lambda e, c=c, cols=cols: e.tensor_sub(
                    a1[c][:, cols], ttmp[:, :], pb[7][:, :]), inc=s_dve)
            consumed_bank(6, s_dve)
            consumed_bank(7, s_dve)

    def proj_group(bank, slot, src_chunks, cols):
        pe_begin_bank(bank)
        R.wait("tensor", w_sem[slot], ring_state[slot]["loaded"])
        join("tensor", s_dve, s_act)
        for fc in range(NCH):
            last = fc == NCH - 1
            R.op("tensor", lambda e, b=bank, s=slot, fc=fc, cols=cols, last=last,
                 src_chunks=src_chunks: e.matmul(
                pb[b][:, :], wring[s][:, fc * 128:(fc + 1) * 128],
                src_chunks[fc][:, cols], start=(fc == 0), stop=last),
                inc=(s_pe if last else None))
        pe_end_bank(bank)
        ring_used(slot)

    def emit_qkv_txl():
        for oc in range(16):
            slot = wload_cols(Wqkv, oc * 128, 128, D, 3 * D)
            for tc in range(NTC):
                bank = (oc * NTC + tc) % 4
                proj_group(bank, slot, a1, slice(tc * 512, (tc + 1) * 512))
                qs = (oc * NTC + tc) % 2
                if oc < 8:
                    for dram, boff in ((qu_d, 0), (qv_d, 8)):
                        consume_bank(bank, "scalar")
                        join("scalar", d_st)
                        R.op("scalar", lambda e, b=bank, qs=qs, oc=oc, boff=boff: e.activation(
                            qstage[qs][:, :], pb[b][:, :], AF.Identity,
                            bias=uvb_sb[:, boff + oc:boff + oc + 1],
                            scale=cvals[:, 2:3]), inc=s_act)
                        consumed_bank(bank, s_act)
                        join("sync", s_act)
                        R.op("sync", lambda e, d=dram, oc=oc, tc=tc, qs=qs: e.dma_start(
                            bass.AP(d.tensor, oc * 128 * S + tc * 512, [[S, 128], [1, 512]]),
                            qstage[qs][:, :]), inc=d_st)
                else:
                    consume_bank(bank, "scalar")
                    join("scalar", d_st)
                    R.op("scalar", lambda e, b=bank, qs=qs: e.copy(qstage[qs][:, :], pb[b][:, :]), inc=s_act)
                    consumed_bank(bank, s_act)
                    join("sync", s_act)
                    R.op("sync", lambda e, oc=oc, tc=tc, qs=qs: e.dma_start(
                        bass.AP(kk_d.tensor, (oc - 8) * 128 * S + tc * 512, [[S, 128], [1, 512]]),
                        qstage[qs][:, :]), inc=d_st)

    def emit_qkv_v(WsrcT=None):
        join("gpsimd", s_pe, s_act)
        for vc in range(2):
            base = 16 if vc == 0 else 24
            for fc in range(NCH):
                if WsrcT is None:
                    src = Wqkv[fc * 128:(fc + 1) * 128, 2 * D + vc * 512:2 * D + (vc + 1) * 512]
                else:
                    src = WsrcT[fc * 128:(fc + 1) * 128, vc * 512:(vc + 1) * 512]
                R.op("gpsimd", lambda e, fc=fc, base=base, src=src: e.dma_start(
                    mid[base + fc][:, :], src), inc=d_mid)
        vw = d_mid.n
        for tt in range(16):
            for vc in range(2):
                base = 16 if vc == 0 else 24
                bank = (tt * 2 + vc) % 4
                pe_begin_bank(bank)
                R.wait("tensor", d_mid, vw)
                join("tensor", s_dve)
                for fc in range(NCH):
                    last = fc == NCH - 1
                    R.op("tensor", lambda e, b=bank, fc=fc, tt=tt, base=base, last=last: e.matmul(
                        pb[b][:, :], a1[fc][:, tt * 128:(tt + 1) * 128],
                        mid[base + fc][:, :], start=(fc == 0), stop=last),
                        inc=(s_pe if last else None))
                pe_end_bank(bank)
                qs = (tt * 2 + vc) % 2
                consume_bank(bank, "scalar")
                R.wait("scalar", qs_sem[qs], qs_sem[qs].n)
                R.op("scalar", lambda e, b=bank, qs=qs: e.copy(qstage[qs][:, :], pb[b][:, :]), inc=s_act)
                consumed_bank(bank, s_act)
                join("sync", s_act)
                R.op("sync", lambda e, tt=tt, vc=vc, qs=qs: e.dma_start(
                    bass.AP(v_d.tensor, tt * 128 * D + vc * 512, [[D, 128], [1, 512]]),
                    qstage[qs][:, :]), inc=qs_sem[qs])

    def emit_qkv_rope():
        for WT, dram, scl in ((WqT, qu_d, "q"), (WkT, kk_d, "k")):
            for oc in range(8):
                slot = wload_cols(WT, oc * 128, 128, D, D)
                for tc in range(NTC):
                    bank = (oc * NTC + tc) % 3
                    proj_group(bank, slot, a1, slice(tc * 512, (tc + 1) * 512))
                    qs = (oc * NTC + tc) % 2
                    consume_bank(bank, "scalar")
                    join("scalar", s_pe, s_dve)
                    R.wait("scalar", qs_sem[qs], qs_sem[qs].n)
                    scl_ap = cvals[:, 3:4] if scl == "q" else cvals[:, 5:6]
                    R.op("scalar", lambda e, b=bank, qs=qs, sa=scl_ap: e.activation(
                        qstage[qs][:, :], pb[b][:, :], AF.Identity, bias=cvals[:, 6:7], scale=sa), inc=s_act)
                    pe_begin_bank(4)
                    join("tensor", s_act)
                    R.op("tensor", lambda e, qs=qs: e.matmul(
                        pb[4][:, :], pswap_sb[:, :], qstage[qs][:, :], start=True, stop=True), inc=s_pe)
                    pe_end_bank(4)
                    consumed_bank(bank, s_pe)
                    consume_bank(4, "vector")
                    tcs = slice(tc * 512, (tc + 1) * 512)
                    R.wait("vector", rts, rts.n)
                    R.op("vector", lambda e, tcs=tcs: e.tensor_mul(rtmp[0][:, :], pb[4][:, :], sinm_sb[:, tcs]))
                    R.op("vector", lambda e, qs=qs, tcs=tcs: e.tensor_mul(rtmp[1][:, :], qstage[qs][:, :], cos_sb[:, tcs]))
                    R.op("vector", lambda e: e.tensor_add(rtmp[0][:, :], rtmp[0][:, :], rtmp[1][:, :]), inc=s_dve)
                    consumed_bank(4, s_dve)
                    join("sync", s_dve)
                    R.op("sync", lambda e, d=dram, oc=oc, tc=tc: e.dma_start(
                        bass.AP(d.tensor, oc * 128 * S + tc * 512, [[S, 128], [1, 512]]),
                        rtmp[0][:, :]), inc=rts)

    def emit_attention(txl, epilogue):
        head_guard = {0: 0, 1: 0, 2: 0}
        voh_guard = {0: 0, 1: 0}
        bdq_free = {0: s_pe.n, 1: s_pe.n}
        reg_rb_mark = {0: (0, 0), 1: (0, 0)}
        rb_mark = {}
        hctx_guard = [s_pe.n]
        ctx_guard = [s_pe.n]
        join("sync", d_st, qs_sem[0], qs_sem[1], rts)

        def nk_of(qc):
            return 4 * (qc + 1)

        def head_load(it):
            qc, h = it // 16, it % 16
            kb = it % 3
            R.wait("sync", s_pe, head_guard[kb])
            sm = hl_sem[kb]
            R.op("sync", lambda e, h=h, kb=kb: e.dma_start(
                kr_b[kb][0:64, :], kk_d[h * 64:(h + 1) * 64, :]), inc=sm)
            if txl:
                R.op("sync", lambda e, h=h, kb=kb: e.dma_start(
                    kr_b[kb][64:128, :], r_d[h * 64:(h + 1) * 64, :]), inc=sm)
            R.op("sync", lambda e, h=h, kb=kb, qc=qc: e.dma_start(
                quv_b[kb][0:64, :], qu_d[h * 64:(h + 1) * 64, qc * 512:(qc + 1) * 512]), inc=sm)
            if txl:
                R.op("sync", lambda e, h=h, kb=kb, qc=qc: e.dma_start(
                    quv_b[kb][64:128, :], qv_d[h * 64:(h + 1) * 64, qc * 512:(qc + 1) * 512]), inc=sm)

        def voh_load(it):
            qc, h = it // 16, it % 16
            vb = it % 2
            R.wait("sync", s_pe, voh_guard[vb])
            for g4 in range(4):
                src = bass.AP(v_d.tensor, (g4 * 4) * 128 * D + h * 64,
                              [[D, 128], [128 * D, 4], [1, 64]])
                dst = bass.AP(voh_b[vb].ap().tensor, (g4 * 4) * 65,
                              [[H * 65, 128], [65, 4], [1, 64]])
                R.op("sync", lambda e, d=dst, s=src: e.dma_start(d, s), inc=vo_sem[vb])

        def emit_M(it):
            qc, h = it // 16, it % 16
            reg = it % 2
            kb = it % 3
            jlo = 1536 - 512 * qc
            njc = qc + 1
            R.wait("tensor", hl_sem[kb], head_ld_mark[it])
            R.wait("sync", mr_sem[0], reg_rb_mark[reg][0])
            R.wait("sync", mr_sem[1], reg_rb_mark[reg][1])
            for qs in range(4):
                for jc in range(njc):
                    bank = 3 + (qs * njc + jc) % 2
                    pe_begin_bank(bank)
                    R.op("tensor", lambda e, b=bank, kb=kb, qs=qs, jl=jlo + jc * 512: e.matmul(
                        pb[b][:, :], quv_b[kb][64:128, qs * 128:(qs + 1) * 128],
                        kr_b[kb][64:128, jl:jl + 512], start=True, stop=True), inc=s_pe)
                    pe_end_bank(bank)
                    ms = (qs * njc + jc) % 2
                    consume_bank(bank, "vector")
                    join("vector", d_mw)
                    R.op("vector", lambda e, b=bank, ms=ms: e.tensor_copy(mstage[ms][:, :], pb[b][:, :]), inc=s_dve)
                    consumed_bank(bank, s_dve)
                    join("sync", s_dve)
                    dst = bass.AP(M_d.tensor, reg * 512 * MW + qs * 128 * MW + jlo + jc * 512,
                                  [[MW, 128], [1, 512]])
                    R.op("sync", lambda e, d=dst, ms=ms: e.dma_start(d, mstage[ms][:, :]), inc=d_mw)

        def issue_rb(it, g):
            qc = it // 16
            reg = it % 2
            p = g % 2
            R.wait("sync", s_pe, bdq_free[p])
            for qs in range(4):
                off = reg * 512 * MW + qs * 128 * MW + (2047 - qc * 512 - qs * 128 + 512 * g)
                src = bass.AP(M_d.tensor, off, [[MW - 1, 128], [1, 512]])
                R.op("sync", lambda e, s=src, g=g, qs=qs: e.dma_start(
                    bdq[p * 4 + qs][:, :], s), inc=mr_sem[p])
            rb_mark[g] = mr_sem[p].n

        def emit_head(it):
            qc, h = it // 16, it % 16
            nk = nk_of(qc)
            kb = it % 3
            vb = it % 2
            R.wait("tensor", hl_sem[kb], head_ld_mark[it])
            R.wait("tensor", vo_sem[vb], voh_ld_mark[it])
            def emit_ST(kt):
                g = kt // 4
                stb = kt % 2
                if txl and kt % 4 == 0 and g + 1 <= (nk - 1) // 4:
                    issue_rb(it, g + 1)
                pe_begin_bank(stb)
                R.op("tensor", lambda e, b=stb, kb=kb, kt=kt, st=(not txl): e.matmul(
                    pb[b][:, :], kr_b[kb][0:64, kt * 128:(kt + 1) * 128],
                    quv_b[kb][0:64, :], start=True, stop=(not txl),
                    skip_group_check=True), inc=(None if txl else s_pe))
                if txl:
                    R.wait("tensor", mr_sem[g % 2], rb_mark[g])
                    kl = (kt % 4) * 128
                    for qs in range(4):
                        R.op("tensor", lambda e, b=stb, g=g, qs=qs, kl=kl: e.matmul(
                            pb[b][:, qs * 128:(qs + 1) * 128],
                            bdq[(g % 2) * 4 + qs][:, kl:kl + 128],
                            id_sb[:, :], start=False, stop=(qs == 3),
                            is_transpose=True, skip_group_check=True),
                            inc=(s_pe if qs == 3 else None))
                pe_end_bank(stb)
                if txl:
                    bdq_free[g % 2] = s_pe.n

            def emit_PX(kt):
                stb = kt % 2
                consume_bank(stb, "scalar")
                join("scalar", s_pool)
                R.op("scalar", lambda e, b=stb, p=kt % 2: e.activation(
                    ptile[p][:, :], pb[b][:, :], AF.Exp), inc=s_act)
                consumed_bank(stb, s_act)
                use_p = kt % 2
                if kt >= nk - 4:
                    delta = kt * 128 - qc * 512
                    join("gpsimd", s_act)
                    R.wait("gpsimd", s_pe, ctx_guard[0])
                    R.op("gpsimd", lambda e, p=kt % 2, delta=delta: e.affine_select(
                        ptile[2][:, :], ptile[p][:, :], [[1, 512]], ALU.is_ge,
                        0.0, base=-delta, channel_multiplier=-1), inc=s_pool)
                    use_p = 2
                pe_begin_bank(2)
                if use_p == 2:
                    join("tensor", s_pool)
                else:
                    join("tensor", s_act)
                R.op("tensor", lambda e, vb=vb, h=h, p=use_p, kt=kt, nk=nk: e.matmul(
                    pb[2][0:65, :], voh_b[vb][:, kt * 65:(kt + 1) * 65],
                    ptile[p][:, :], start=(kt == 0), stop=(kt == nk - 1),
                    skip_group_check=True), inc=s_pe)
                if use_p == 2:
                    ctx_guard[0] = s_pe.n
                pe_end_bank(2)

            emit_ST(0)
            for kt in range(1, nk):
                emit_ST(kt)
                emit_PX(kt - 1)
            emit_PX(nk - 1)
            consume_bank(2, "vector")
            R.op("vector", lambda e: e.reciprocal(recip_row[:, :], pb[2][64:65, :]), inc=s_dve)
            pe_begin_bank(5)
            R.wait("tensor", s_dve, s_dve.n)
            R.op("tensor", lambda e: e.matmul(pb[5][0:64, :], ones_f[0:1, 0:64],
                                              recip_row[0:1, :], start=True, stop=True), inc=s_pe)
            pe_end_bank(5)
            consume_bank(5, "scalar")
            join("scalar", s_dve)
            R.op("scalar", lambda e: e.copy(bcsb[:, :], pb[5][0:64, :]), inc=s_act)
            consumed_bank(5, s_act)
            ch, half = h // 2, (h % 2) * 64
            join("vector", s_act)
            R.wait("vector", s_pe, hctx_guard[0])
            R.op("vector", lambda e, ch=ch, half=half: e.tensor_mul(
                hctx[ch][half:half + 64, :], pb[2][0:64, :], bcsb[:, :]), inc=s_dve)
            consumed_bank(2, s_dve)

        head_ld_mark = {}
        voh_ld_mark = {}

        head_load(0); head_ld_mark[0] = hl_sem[0].n
        voh_load(0); voh_ld_mark[0] = vo_sem[0].n
        if txl:
            emit_M(0)
            join("sync", d_mw)
            issue_rb(0, 0)
        head_load(1); head_ld_mark[1] = hl_sem[1].n
        voh_load(1); voh_ld_mark[1] = vo_sem[1].n

        for it in range(64):
            qc, h = it // 16, it % 16
            if txl and it + 1 < 64:
                emit_M(it + 1)
            if it + 2 < 64:
                head_load(it + 2); head_ld_mark[it + 2] = hl_sem[(it + 2) % 3].n
            emit_head(it)
            head_guard[it % 3] = s_pe.n
            voh_guard[it % 2] = s_pe.n
            if it + 2 < 64:
                voh_load(it + 2); voh_ld_mark[it + 2] = vo_sem[it % 2].n
            if txl:
                reg_rb_mark[it % 2] = (mr_sem[0].n, mr_sem[1].n)
            if txl and it + 1 < 64:
                join("sync", d_mw)
                issue_rb(it + 1, 0)
            if h == 15:
                epilogue(qc)
                hctx_guard[0] = s_pe.n

    def ffn_epilogue(qc):
        cols = slice(qc * 512, (qc + 1) * 512)
        for oc in range(NCH):
            slot = wload_cols(Wo, oc * 128, 128, D, D)
            bank = oc % 2
            pe_begin_bank(bank)
            R.wait("tensor", w_sem[slot], ring_state[slot]["loaded"])
            join("tensor", s_dve)
            for dc in range(NCH):
                last = dc == NCH - 1
                R.op("tensor", lambda e, b=bank, s=slot, dc=dc, last=last: e.matmul(
                    pb[b][:, :], wring[s][:, dc * 128:(dc + 1) * 128],
                    hctx[dc][:, :], start=(dc == 0), stop=last),
                    inc=(s_pe if last else None))
            pe_end_bank(bank)
            ring_used(slot)
            consume_bank(bank, "vector")
            R.op("vector", lambda e, b=bank, oc=oc, cols=cols: e.tensor_add(
                a1[oc][:, cols], pb[b][:, :], a1[oc][:, cols]), inc=s_dve)
            consumed_bank(bank, s_dve)
        emit_ln_inplace(cols, 0, 8)

    def emit_ln_inplace(cols, gcol, bcol):
        join("scalar", s_dve)
        for c in range(NCH):
            ln_stats_chunk(a1[c][:, cols], c, sqst[c % 2][:, :])
        rv = ln_rows()
        ln_bcast(rv)
        for c in range(NCH):
            consume_bank(6, "vector")
            consume_bank(7, "vector")
            join("vector", s_act)
            R.op("vector", lambda e, c=c, cols=cols: e.tensor_mul(ttmp[:, :], a1[c][:, cols], pb[6][:, :]))
            R.op("vector", lambda e, c=c: e.tensor_sub(xstage[0][:, :], ttmp[:, :], pb[7][:, :]), inc=s_dve)
            join("scalar", s_dve)
            R.op("scalar", lambda e, c=c, cols=cols, g=gcol, b=bcol: e.activation(
                a1[c][:, cols], xstage[0][:, :], AF.Identity,
                bias=lnp_sb[:, b + c:b + c + 1], scale=lnp_sb[:, g + c:g + c + 1]), inc=s_act)
        consumed_bank(6, s_dve)
        consumed_bank(7, s_dve)

    def attn_epilogue(qc):
        cols = slice(qc * 512, (qc + 1) * 512)
        for oc in range(NCH):
            slot = wload_cols(WoaT, oc * 128, 128, D, D)
            bank = oc % 2
            pe_begin_bank(bank)
            R.wait("tensor", w_sem[slot], ring_state[slot]["loaded"])
            join("tensor", s_dve)
            for dc in range(NCH):
                last = dc == NCH - 1
                R.op("tensor", lambda e, b=bank, s=slot, dc=dc, last=last: e.matmul(
                    pb[b][:, :], wring[s][:, dc * 128:(dc + 1) * 128],
                    hctx[dc][:, :], start=(dc == 0), stop=last),
                    inc=(s_pe if last else None))
            pe_end_bank(bank)
            ring_used(slot)
            consume_bank(bank, "scalar")
            ap_ = oc % 2
            R.wait("scalar", ac_sem[ap_], ac_sem[ap_].n)
            R.op("scalar", lambda e, b=bank, oc=oc, ap_=ap_: e.activation(
                accst[ap_][:, :], pb[b][:, :], AF.Identity,
                bias=uvb_sb[:, 16 + oc:16 + oc + 1], scale=cvals[:, 5:6]), inc=s_act)
            consumed_bank(bank, s_act)
            join("gpsimd", s_act, d_st)
            R.op("gpsimd", lambda e, oc=oc, cols=cols, ap_=ap_: e.dma_start(
                yout[oc * 128:(oc + 1) * 128, cols], accst[ap_][:, :],
                accum_op=ALU.add), inc=ac_sem[ap_])

    def ff1v(oc, ts):
        if oc < 16:
            return mid[oc * 2 + ts][:, :]
        if oc < 24:
            t = bdq[oc - 16].ap().bitcast(BF16)
            return t[:, ts * 512:(ts + 1) * 512]
        return hid2[oc - 24][:, ts * 512:(ts + 1) * 512]

    def emit_ff():
        for th in range(2):
            base = th * 1024
            for oc in range(32):
                slot = wload_cols(Wff1, oc * 128, 128, D, FFD)
                for ts in range(2):
                    cols = slice(base + ts * 512, base + (ts + 1) * 512)
                    bank = (oc * 2 + ts) % 2
                    proj_group(bank, slot, a1, cols)
                    consume_bank(bank, "scalar")
                    join("scalar", s_pe)
                    R.op("scalar", lambda e, b=bank, oc=oc, ts=ts: e.activation(
                        ff1v(oc, ts), pb[b][:, :], AF.Relu,
                        bias=bff1_sb[:, oc:oc + 1], scale=cvals[:, 5:6]), inc=s_act)
                    consumed_bank(bank, s_act)
            for dc in range(NCH):
                slots = [wload_cols(Wff2, dc * 128, 128, 1024, D, rowoff=rg * 1024)
                         for rg in range(4)]
                for ts in range(2):
                    cols = slice(base + ts * 512, base + (ts + 1) * 512)
                    bank = 2 + (dc * 2 + ts) % 4
                    pe_begin_bank(bank)
                    for rg in range(4):
                        R.wait("tensor", w_sem[slots[rg]], ring_state[slots[rg]]["loaded"])
                    join("tensor", s_act)
                    for oc in range(32):
                        last = oc == 31
                        sl = slots[oc // 8]
                        R.op("tensor", lambda e, b=bank, sl=sl, oc=oc, ts=ts, last=last: e.matmul(
                            pb[b][:, :], wring[sl][:, (oc % 8) * 128:(oc % 8 + 1) * 128],
                            ff1v(oc, ts), start=(oc == 0), stop=last),
                            inc=(s_pe if last else None))
                    pe_end_bank(bank)
                    for rg in range(4):
                        ring_used(slots[rg])
                    consume_bank(bank, "scalar")
                    join("scalar", s_dve)
                    qp = ts
                    R.op("scalar", lambda e, b=bank, dc=dc, qp=qp: e.activation(
                        qstage[qp][:, :], pb[b][:, :], AF.Identity,
                        bias=uvb_sb[:, 24 + dc:24 + dc + 1], scale=cvals[:, 5:6]), inc=s_act)
                    consumed_bank(bank, s_act)
                    join("vector", s_act)
                    R.op("vector", lambda e, dc=dc, cols=cols, qp=qp: e.tensor_add(
                        a1[dc][:, cols], qstage[qp][:, :], a1[dc][:, cols]), inc=s_dve)
            for ts in range(2):
                emit_ln_inplace(slice(base + ts * 512, base + (ts + 1) * 512), 16, 24)

    rt_pe_mark = [0]

    def emit_routing():
        t_rt = None
        if "t_rt" in taps:
            t_rt = nc.dram_tensor("t_rt", [128, 256], F32, kind="ExternalOutput").ap()
        for tt in range(16):
            bank = tt % 2
            pe_begin_bank(bank)
            join("tensor", s_dve, s_act)
            for fc in range(NCH):
                last = fc == NCH - 1
                R.op("tensor", lambda e, b=bank, fc=fc, tt=tt, last=last: e.matmul(
                    pb[b][0:128, 0:8], a1[fc][:, tt * 128:(tt + 1) * 128],
                    wg_sb[:, fc * 8:(fc + 1) * 8], start=(fc == 0), stop=last,
                    skip_group_check=True), inc=(s_pe if last else None))
            pe_end_bank(bank)
            consume_bank(bank, "vector")
            join("vector", s_act, d_st)
            R.wait("vector", s_pe, rt_pe_mark[0])
            R.op("vector", lambda e, b=bank: e.tensor_copy(rt[:, 0:8], pb[b][:, 0:8]), inc=s_dve)
            consumed_bank(bank, s_dve)
            vsync()
            R.op("vector", lambda e: e.reduce_max(rt[:, 8:9], rt[:, 0:8], axis=AX.X, negate=True), inc=s_dve)
            vsync()
            R.op("vector", lambda e: e.reduce_max(rt[:, 13:14], rt[:, 0:8], axis=AX.X), inc=s_dve)
            join("scalar", s_dve)
            R.op("scalar", lambda e: e.activation(rt[:, 16:24], rt[:, 0:8], AF.Exp,
                                                  bias=rt[:, 8:9], scale=cvals[:, 5:6]), inc=s_act)
            ssync()
            R.op("scalar", lambda e: e.activation(rt[:, 24:32], rt[:, 0:8], AF.Identity,
                                                  bias=rt[:, 13:14], scale=cvals[:, 6:7]), inc=s_act)
            join("vector", s_act)
            R.op("vector", lambda e: e.tensor_tensor(rt[:, 32:40], rt[:, 0:8], rt[:, 24:32], ALU.is_equal), inc=s_dve)
            vsync()
            R.op("vector", lambda e: e.reduce_sum(rt[:, 9:10], rt[:, 16:24], axis=AX.X), inc=s_dve)
            vsync()
            R.op("vector", lambda e: e.reciprocal(rt[:, 10:11], rt[:, 9:10]), inc=s_dve)
            vsync()
            R.op("vector", lambda e: e.scalar_tensor_tensor(
                rt[:, 40:48], rt[:, 32:40], 16384.0, rt[:, 56:64], ALU.mult, ALU.add), inc=s_dve)
            vsync()
            R.op("vector", lambda e: e.reduce_max(rt[:, 11:12], rt[:, 40:48], axis=AX.X), inc=s_dve)
            join("scalar", s_dve)
            R.op("scalar", lambda e: e.activation(rt[:, 40:48], rt[:, 48:56], AF.Square,
                                                  bias=rt[:, 11:12], scale=cvals[:, 5:6]), inc=s_act)
            ssync()
            R.op("scalar", lambda e: e.activation(rt[:, 32:40], rt[:, 40:48], AF.Relu,
                                                  bias=cvals[:, 5:6], scale=cvals[:, 4:5]), inc=s_act)
            ssync()
            R.op("scalar", lambda e: e.activation(rt[:, 24:32], rt[:, 32:40], AF.Identity,
                                                  bias=cvals[:, 6:7], scale=rt[:, 10:11]), inc=s_act)
            if t_rt is not None:
                join("sync", s_act, s_dve)
                R.op("sync", lambda e, tt=tt, t_rt=t_rt: e.dma_start(t_rt[:, tt * 16:(tt + 1) * 16], rt[:, 0:16]), inc=d_st)
            pe_begin_bank(5)
            join("tensor", s_act)
            R.op("tensor", lambda e: e.matmul(pb[5][0:8, 0:128], rt[:, 24:32], id_sb[:, :],
                                              start=True, stop=True, is_transpose=True), inc=s_pe)
            pe_end_bank(5)
            rt_pe_mark[0] = s_pe.n
            consume_bank(5, "scalar")
            join("scalar", s_dve)
            R.op("scalar", lambda e, tt=tt: e.copy(wT_bf[0:8, tt * 128:(tt + 1) * 128], pb[5][0:8, 0:128]), inc=s_act)
            consumed_bank(5, s_act)

    def emit_moe():
        first_th = True
        for th in range(2):
            base = th * 1024
            for e_ in range(NE):
                for oc in range(32):
                    slot = wload_cols(We1, e_ * D * FFD + oc * 128, 128, D, FFD)
                    for ts in range(2):
                        cols = slice(base + ts * 512, base + (ts + 1) * 512)
                        bank = (oc * 2 + ts) % 2
                        proj_group(bank, slot, a1, cols)
                        consume_bank(bank, "scalar")
                        join("scalar", s_pe)
                        R.op("scalar", lambda e, b=bank, e_=e_, oc=oc, ts=ts: e.activation(
                            ff1v(oc, ts), pb[b][:, :], AF.Relu,
                            bias=be1_sb[:, e_ * 32 + oc:e_ * 32 + oc + 1], scale=cvals[:, 5:6]), inc=s_act)
                        consumed_bank(bank, s_act)
                for ts in range(2):
                    pe_begin_bank(5)
                    join("tensor", s_act)
                    R.op("tensor", lambda e, e_=e_, base=base, ts=ts: e.matmul(
                        pb[5][:, :], eye8_sb[:, e_ * 128:(e_ + 1) * 128],
                        wT_bf[0:8, base + ts * 512:base + (ts + 1) * 512],
                        start=True, stop=True), inc=s_pe)
                    pe_end_bank(5)
                    consume_bank(5, "scalar")
                    join("scalar", s_dve)
                    R.op("scalar", lambda e, ts=ts: e.copy(wbc_sb[:, ts * 512:(ts + 1) * 512], pb[5][:, :]), inc=s_act)
                    consumed_bank(5, s_act)
                if first_th and e_ == 0 and "t_wbc" in taps:
                    t_wbc = nc.dram_tensor("t_wbc", [128, 1024], BF16, kind="ExternalOutput").ap()
                    join("sync", s_act)
                    R.op("sync", lambda e, t=t_wbc: e.dma_start(t[:, :], wbc_sb[:, :]), inc=d_st)
                    join("gpsimd", d_st)
                for dc in range(NCH):
                    slots = [wload_cols(We2, e_ * FFD * D + dc * 128, 128, 1024, D,
                                        rowoff=rg * 1024) for rg in range(4)]
                    for ts in range(2):
                        bank = 2 + (dc * 2 + ts) % 4
                        pe_begin_bank(bank)
                        for rg in range(4):
                            R.wait("tensor", w_sem[slots[rg]], ring_state[slots[rg]]["loaded"])
                        join("tensor", s_act)
                        for oc in range(32):
                            last = oc == 31
                            sl = slots[oc // 8]
                            R.op("tensor", lambda e, b=bank, sl=sl, oc=oc, ts=ts, last=last: e.matmul(
                                pb[b][:, :], wring[sl][:, (oc % 8) * 128:(oc % 8 + 1) * 128],
                                ff1v(oc, ts), start=(oc == 0), stop=last),
                                inc=(s_pe if last else None))
                        pe_end_bank(bank)
                        for rg in range(4):
                            ring_used(slots[rg])
                        consume_bank(bank, "scalar")
                        join("scalar", s_dve)
                        qp = ts
                        R.op("scalar", lambda e, b=bank, dc=dc, e_=e_, qp=qp: e.activation(
                            qstage[qp][:, :], pb[b][:, :], AF.Identity,
                            bias=be2_sb[:, e_ * 8 + dc:e_ * 8 + dc + 1], scale=cvals[:, 5:6]), inc=s_act)
                        consumed_bank(bank, s_act)
                        join("vector", s_act)
                        pc = slice(ts * 512, (ts + 1) * 512)
                        if first_th and e_ == 0 and dc == 0 and ts == 0 and "t_wbc" in taps:
                            t_qs = nc.dram_tensor("t_qs", [128, 512], BF16, kind="ExternalOutput").ap()
                            join("sync", s_act)
                            R.op("sync", lambda e, t=t_qs, qp=qp: e.dma_start(t[:, :], qstage[qp][:, :]), inc=d_st)
                            join("vector", d_st)
                        if e_ == 0:
                            R.op("vector", lambda e, dc=dc, pc=pc, ts=ts, qp=qp: e.tensor_mul(
                                pacc[dc][:, pc], qstage[qp][:, :],
                                wbc_sb[:, ts * 512:(ts + 1) * 512]), inc=s_dve)
                        else:
                            R.op("vector", lambda e, ts=ts, qp=qp: e.tensor_mul(
                                rtmp[0][:, :], qstage[qp][:, :],
                                wbc_sb[:, ts * 512:(ts + 1) * 512]))
                            R.op("vector", lambda e, dc=dc, pc=pc: e.tensor_add(
                                pacc[dc][:, pc], pacc[dc][:, pc], rtmp[0][:, :]), inc=s_dve)
            for ts in range(2):
                cols = slice(base + ts * 512, base + (ts + 1) * 512)
                pc = slice(ts * 512, (ts + 1) * 512)
                for dc in range(NCH):
                    R.op("vector", lambda e, dc=dc, pc=pc, cols=cols: e.tensor_add(
                        pacc[dc][:, pc], pacc[dc][:, pc], a1[dc][:, cols]), inc=s_dve)
                join("scalar", s_dve)
                for c in range(NCH):
                    ln_stats_chunk(pacc[c][:, pc], c, sqst[c % 2][:, :])
                rv = ln_rows()
                ln_bcast(rv)
                for c in range(NCH):
                    consume_bank(6, "vector")
                    consume_bank(7, "vector")
                    join("vector", s_act)
                    R.op("vector", lambda e, c=c, pc=pc: e.tensor_mul(ttmp[:, :], pacc[c][:, pc], pb[6][:, :]))
                    R.op("vector", lambda e: e.tensor_sub(xstage[0][:, :], ttmp[:, :], pb[7][:, :]), inc=s_dve)
                    join("scalar", s_dve)
                    R.op("scalar", lambda e, c=c, cols=cols: e.activation(
                        a1[c][:, cols], xstage[0][:, :], AF.Identity,
                        bias=lnp_sb[:, 40 + c:40 + c + 1], scale=lnp_sb[:, 32 + c:32 + c + 1]), inc=s_act)
                consumed_bank(6, s_dve)
                consumed_bank(7, s_dve)
                for oc in range(NCH):
                    slot = wload_cols(Wout, oc * 128, 128, D, D)
                    bank = oc % 4
                    proj_group(bank, slot, a1, cols)
                    consume_bank(bank, "scalar")
                    ap_ = oc % 2
                    R.wait("scalar", ac_sem[ap_], ac_sem[ap_].n)
                    R.op("scalar", lambda e, b=bank, ap_=ap_: e.copy(accst[ap_][:, :], pb[b][:, :]), inc=s_act)
                    consumed_bank(bank, s_act)
                    join("gpsimd", s_act, d_st)
                    R.op("gpsimd", lambda e, oc=oc, cols=cols, ap_=ap_: e.dma_start(
                        yout[oc * 128:(oc + 1) * 128, cols], accst[ap_][:, :],
                        accum_op=ALU.add), inc=ac_sem[ap_])
            if first_th and "t_p3a" in taps:
                tap_pacc("t_p3a")
            first_th = False

    def tap_a1(name):
        t = nc.dram_tensor(name, [D, S], F32, kind="ExternalOutput").ap()
        join("vector", s_act, s_pe)
        for c in range(NCH):
            for tc in range(NTC):
                join("vector", d_st)
                R.op("vector", lambda e, c=c, tc=tc: e.tensor_copy(
                    ttmp[:, :], a1[c][:, tc * 512:(tc + 1) * 512]), inc=s_dve)
                join("sync", s_dve)
                R.op("sync", lambda e, t=t, c=c, tc=tc: e.dma_start(
                    bass.AP(t.tensor, c * 128 * S + tc * 512, [[S, 128], [1, 512]]),
                    ttmp[:, :]), inc=d_st)

    def tap_y(name):
        t = nc.dram_tensor(name, [D, S], F32, kind="ExternalOutput").ap()
        join("sync", ac_sem[0], ac_sem[1], d_st, s_act, s_dve)
        for c in range(NCH):
            for tc in range(NTC):
                p = (c + tc) % 2
                R.op("sync", lambda e, c=c, tc=tc, p=p: e.dma_start(
                    xstage[p][:, :], yout[c * 128:(c + 1) * 128, tc * 512:(tc + 1) * 512]), inc=d_ld)
                join("sync", d_ld)
                R.op("sync", lambda e, t=t, c=c, tc=tc, p=p: e.dma_start(
                    bass.AP(t.tensor, c * 128 * S + tc * 512, [[S, 128], [1, 512]]),
                    xstage[p][:, :]), inc=d_st)

    def tap_hctx(name):
        t = nc.dram_tensor(name, [D, 512], F32, kind="ExternalOutput").ap()
        join("vector", s_act, s_pe, s_dve)
        for c in range(NCH):
            join("vector", d_st)
            R.op("vector", lambda e, c=c: e.tensor_copy(ttmp[:, :], hctx[c][:, :]), inc=s_dve)
            join("sync", s_dve)
            R.op("sync", lambda e, t=t, c=c: e.dma_start(
                bass.AP(t.tensor, c * 128 * 512, [[512, 128], [1, 512]]), ttmp[:, :]), inc=d_st)

    def tap_wt(name):
        t = nc.dram_tensor(name, [NE, S], BF16, kind="ExternalOutput").ap()
        join("sync", s_act, s_dve)
        R.op("sync", lambda e, t=t: e.dma_start(t[:, :], wT_bf[:, :]), inc=d_st)

    def tap_pacc(name):
        t = nc.dram_tensor(name, [D, 1024], BF16, kind="ExternalOutput").ap()
        join("sync", s_dve, s_act)
        for c in range(NCH):
            R.op("sync", lambda e, t=t, c=c: e.dma_start(
                t[c * 128:(c + 1) * 128, :], pacc[c][:, :]), inc=d_st)

    join("tensor", s_pool)

    def ffn_call(first):
        emit_ln0()
        if first and "t_h" in taps:
            tap_a1("t_h")
        emit_qkv_txl()
        emit_qkv_v()
        emit_attention(True, ffn_epilogue)
        if first and "t_ctx" in taps:
            tap_hctx("t_ctx")
        if first and "t_h1" in taps:
            tap_a1("t_h1")
        emit_ff()
        if first and "t_h2" in taps:
            tap_a1("t_h2")
        emit_routing()
        if first and "t_wt" in taps:
            tap_wt("t_wt")
        emit_moe()
        if first and "t_p3" in taps:
            tap_pacc("t_p3")
        if first and "t_h3" in taps:
            tap_a1("t_h3")

    def attn_call():
        emit_ln0()
        emit_qkv_rope()
        emit_qkv_v(WsrcT=WvT)
        emit_attention(False, attn_epilogue)

    ffn_call(True)
    if "t_x1" in taps:
        tap_y("t_x1")
    if stages >= 2:
        attn_call()
        if "t_x2" in taps:
            tap_y("t_x2")
    if stages >= 3:
        ffn_call(False)

    join("sync", s_dve, s_act, s_pe, s_pool)
    for s_ in (d_st, d_ld, d_mw, d_mr, d_w, d_acc, d_mid, rts,
               *w_sem, *hl_sem, *vo_sem, *xs_sem, *mr_sem, *qs_sem, *ac_sem):
        R.wait("sync", s_)

    with nc.Block() as block:
        R.replay(block)
    stck.close()
    return nc


def kernel(**inputs):
    shared = _prep_shared(inputs)
    x = np.asarray(inputs["L0_attn_x"])
    nc = _build()
    in_maps = []
    for b in range(8):
        m = dict(shared)
        m["xT"] = np.ascontiguousarray(x[b].T)
        in_maps.append(m)
    res = run_bass_kernel_spmd(nc, in_maps, core_ids=list(range(8)))
    out = np.stack([np.asarray(r["y"]).T for r in res.results], axis=0)
    return out.astype(np.float32)
